# version 08
# baseline (speedup 1.0000x reference)
import os
from contextlib import ExitStack

import numpy as np
import ml_dtypes

import concourse.bass as bass
import concourse.mybir as mybir
from concourse.bass_utils import run_bass_kernel_spmd

F32 = mybir.dt.float32
BF16 = mybir.dt.bfloat16
I32 = mybir.dt.int32
AF = mybir.ActivationFunctionType
ALU = mybir.AluOpType
AX = mybir.AxisListType
BF = ml_dtypes.bfloat16

S = 8192
HID = 2048
NH = 16
D = 128
RATIO = 4
C = S // RATIO
WIN = 128
FFN = 5632
NEG = -3.0e38
EPS_LN = 1e-5
EPS_RMS = 1e-6
SCALE = float(1.0 / np.sqrt(np.float64(128.0)))
PI = float(np.pi)
TWO_PI = float(2.0 * np.pi)
LN1E4_32 = float(np.log(10000.0) / 32.0)
NCORE = 8
NT = 64
TPC = 8
NPAIR = 4
NSLOT = 12
HG = 4
NHG = NH // HG
PAIR_W = [512, 1024, 1536, 2048]
TILE_W = [PAIR_W[k] for k in range(NPAIR) for _ in range(2)]
KB = 1024

MEGAB_BYTES = 136 * KB
MEGAF_BYTES = 44 * KB
SMALL_COLS = 2432


def core_pairs(c):
    return [c, 15 - c, 16 + c, 31 - c]


def own_global_tiles(c):
    out = []
    for p in core_pairs(c):
        out += [2 * p, 2 * p + 1]
    return out


def slot_global_tiles(c):
    out = []
    for p in core_pairs(c):
        h = 2 * p - 1
        if h < 0:
            h = 0
        out += [h, 2 * p, 2 * p + 1]
    return out


class Prog:
    def __init__(self, nc):
        self.nc = nc
        self.steps = []
        self.ivals = {}
        self._last_dma = {}

    def _deps_for(self, reads, writes):
        deps = set()
        for (sp, lo, hi) in list(reads) + list(writes):
            for iv in self.ivals.get(sp, []):
                if iv[0] < hi and lo < iv[1] and iv[2] is not None:
                    deps.add(iv[2])
        for (sp, lo, hi) in writes:
            for iv in self.ivals.get(sp, []):
                if iv[0] < hi and lo < iv[1]:
                    deps.update(iv[3])
        return deps

    def _update(self, sid, reads, writes):
        for (sp, lo, hi) in writes:
            lst = self.ivals.setdefault(sp, [])
            keep = []
            for iv in lst:
                if iv[0] < hi and lo < iv[1]:
                    if iv[0] < lo:
                        keep.append([iv[0], lo, iv[2], set(iv[3])])
                    if hi < iv[1]:
                        keep.append([hi, iv[1], iv[2], set(iv[3])])
                else:
                    keep.append(iv)
            keep.append([lo, hi, sid, set()])
            self.ivals[sp] = keep
        for (sp, lo, hi) in reads:
            for iv in self.ivals.get(sp, []):
                if iv[0] < hi and lo < iv[1]:
                    iv[3].add(sid)

    def op(self, eng, fn, reads=(), writes=(), dma_slot=None, tag=""):
        sid = len(self.steps)
        semkey = ("d", dma_slot) if dma_slot is not None else ("e", eng)
        deps = self._deps_for(reads, writes)
        if dma_slot is not None:
            prev = self._last_dma.get(dma_slot)
            if prev is not None:
                deps.add(prev)
            self._last_dma[dma_slot] = sid
        self.steps.append(dict(sid=sid, eng=eng, fn=fn, deps=deps, semkey=semkey,
                               dma=dma_slot is not None, needed=False, val=None,
                               tag=tag))
        self._update(sid, reads, writes)
        return sid

    def finalize(self, block):
        nc = self.nc
        steps = self.steps
        for s in steps:
            for dd in s["deps"]:
                steps[dd]["needed"] = True
        for s in steps:
            if s["dma"]:
                s["needed"] = True
        counts = {}
        for s in steps:
            if s["needed"]:
                inc = 16 if s["dma"] else 1
                v = counts.get(s["semkey"], 0) + inc
                counts[s["semkey"]] = v
                s["val"] = v
        semkeys = sorted(set(s["semkey"] for s in steps if s["needed"]))
        self._sem_ctxs = []
        self.sems = {}
        for i, sk in enumerate(semkeys):
            cm = nc.semaphore(f"s{i}")
            self.sems[sk] = cm.__enter__()
            self._sem_ctxs.append(cm)
        self.final_counts = counts
        per_eng = {e: [] for e in ("tensor", "vector", "scalar", "gpsimd", "sync")}
        for s in steps:
            per_eng[s["eng"]].append(s)
        eng_idx = {}
        cnt = {}
        for s in steps:
            if not s["dma"]:
                k = s["eng"]
                eng_idx[s["sid"]] = cnt.get(k, 0)
                cnt[k] = cnt.get(k, 0) + 1

        def mk_body(eng, mysteps):
            def body(e):
                waited = {}
                drain_cover = 0
                emitted = 0
                for s in mysteps:
                    want = {}
                    need_drain = False
                    for dd in s["deps"]:
                        ds = steps[dd]
                        if (not ds["dma"]) and ds["eng"] == eng:
                            if eng != "tensor" and eng_idx[dd] >= drain_cover:
                                need_drain = True
                            continue
                        sk, v = ds["semkey"], ds["val"]
                        if want.get(sk, 0) < v:
                            want[sk] = v
                    if need_drain:
                        e.drain()
                        drain_cover = emitted
                    for sk in sorted(want):
                        v = want[sk]
                        if waited.get(sk, 0) < v:
                            e.wait_ge(self.sems[sk], v)
                            waited[sk] = v
                    inst = s["fn"](e)
                    if s["needed"]:
                        inst.then_inc(self.sems[s["semkey"]], 16 if s["dma"] else 1)
                    if not s["dma"]:
                        emitted += 1
                if eng == "sync":
                    for sk in sorted(self.final_counts):
                        if sk[0] == "d":
                            v = self.final_counts[sk]
                            if waited.get(sk, 0) < v:
                                e.wait_ge(self.sems[sk], v)
            return body

        block.sync(mk_body("sync", per_eng["sync"]))
        block.tensor(mk_body("tensor", per_eng["tensor"]))
        block.vector(mk_body("vector", per_eng["vector"]))
        block.scalar(mk_body("scalar", per_eng["scalar"]))
        block.gpsimd(mk_body("gpsimd", per_eng["gpsimd"]))


PHASES = ("c", "p0", "p2", "p3", "p4a", "p4b")
PH = {p: 1 << i for i, p in enumerate(PHASES)}
ALL = (1 << len(PHASES)) - 1


def phmask(phases):
    if phases == "all":
        return ALL
    m = 0
    for p in phases.split("+"):
        m |= PH[p]
    return m


class Arena:
    def __init__(self, name, nbytes, align=64):
        self.name = name
        self.nbytes = nbytes
        self.align = align
        self.regions = []

    def alloc(self, name, nbytes, phases):
        mask = phmask(phases)
        nbytes = (nbytes + self.align - 1) // self.align * self.align
        taken = sorted((lo, hi) for lo, hi, m, _ in self.regions if m & mask)
        pos = 0
        for lo, hi in taken:
            if pos + nbytes <= lo:
                break
            pos = max(pos, hi)
        assert pos + nbytes <= self.nbytes, \
            f"{self.name}: out of space for {name} ({nbytes}B, phases={phases})"
        self.regions.append((pos, pos + nbytes, mask, name))
        return pos


def build_program(debug=(), stop_after=None):
    nc = bass.Bass("TRN2", target_bir_lowering=False)
    dbg = set(debug)
    run_p4 = stop_after != "p3"

    def din(name, shape, dt):
        return nc.dram_tensor(name, shape, dt, kind="ExternalInput").ap()

    d = {}
    d["x"] = din("x", [S, HID], F32)
    d["x_own"] = din("x_own", [NSLOT * 128, HID], F32)
    d["posT_own"] = din("posT_own", [32, NSLOT * 128], F32)
    d["kw_own"] = din("kw_own", [32, NSLOT * 128], F32)
    d["posB"] = din("posB", [32, C], F32)
    d["kwB"] = din("kwB", [32, C], F32)
    d["cmax"] = din("cmax", [128, TPC], F32)
    d["lmask"] = din("lmask", [128, TPC * 256], F32)
    for nm, sh in [("Wq", [HID, NH * D]), ("Wcmp", [HID, 1]), ("Wk", [HID, D]),
                   ("Wv", [HID, D]), ("Wlk", [HID, NH * D]), ("Wlv", [HID, NH * D]),
                   ("Wo", [NH * D, HID])]:
        d[nm] = din(nm, sh, F32)
    for nm, sh in [("bq", NH * D), ("bcmp", 1), ("bk", D), ("bv", D),
                   ("blk", NH * D), ("blv", NH * D), ("qn_w", D), ("kn_w", D),
                   ("bo", HID), ("s_gate", FFN), ("s_up", FFN), ("s_down", HID)]:
        d[nm] = din(nm, [sh], F32)
    d["wgT"] = din("wgT", [HID, FFN], I32)
    d["wuT"] = din("wuT", [HID, FFN], I32)
    d["wdT"] = din("wdT", [FFN, HID], I32)

    y = nc.dram_tensor("y", [TPC * 128, HID], F32, kind="ExternalOutput").ap()
    merged_hbm = nc.dram_tensor(
        "merged_scratch", [NH, 128, 1024], BF16,
        kind="ExternalOutput" if "merged" in dbg else "Internal").ap()
    lnf_hbm = nc.dram_tensor("lnf_scratch", [NSLOT, 128, 2048], BF16,
                             kind="Internal").ap()

    dbg_outs = {}

    def dout(name, shape, dt=F32):
        dbg_outs[name] = nc.dram_tensor(name, shape, dt, kind="ExternalOutput").ap()
        return dbg_outs[name]

    with ExitStack() as ctx:
        megaB_h = ctx.enter_context(
            nc.sbuf_tensor("megaB", [128, MEGAB_BYTES // 2], BF16))
        megaF_h = ctx.enter_context(
            nc.sbuf_tensor("megaF", [128, MEGAF_BYTES // 4], F32))
        small_h = ctx.enter_context(nc.sbuf_tensor("small", [128, SMALL_COLS], F32))
        psF_h = ctx.enter_context(nc.psum_tensor("psF", [128, 2560], F32))
        psB_h = ctx.enter_context(nc.psum_tensor("psB", [128, 3072], BF16))

        prog = Prog(nc)
        OP = prog.op
        AB = Arena("megaB", MEGAB_BYTES, align=64)
        AFF = Arena("megaF", MEGAF_BYTES, align=64)

        def MBr(name, cols, phases, part=128):
            lo = AB.alloc(name, cols * 2, phases)
            return megaB_h[0:part, lo // 2: lo // 2 + cols], ("megaB", lo, lo + 2 * cols)

        def MFr(name, cols, phases, part=128):
            lo = AFF.alloc(name, cols * 4, phases)
            return megaF_h[0:part, lo // 4: lo // 4 + cols], ("megaF", lo, lo + 4 * cols)

        def MBrow(name, cols, phases):
            return MBr(name, cols, phases, part=1)

        def MFrow(name, cols, phases):
            return MFr(name, cols, phases, part=1)

        small_off = [0]

        def SC(cols):
            lo = small_off[0]
            small_off[0] += cols
            assert small_off[0] <= SMALL_COLS
            return small_h[:, lo:lo + cols], ("small", lo * 4, (lo + cols) * 4)

        def PF(col_lo, cols):
            return psF_h[:, col_lo:col_lo + cols], \
                ("psF", col_lo * 4, (col_lo + cols) * 4)

        def PB(col_lo, cols):
            return psB_h[:, col_lo:col_lo + cols], \
                ("psB", col_lo * 2, (col_lo + cols) * 2)

        ident_bf, B_identb = MBr("ident_bf", 128, "all")
        ones128, B_ones128 = MBr("ones128", 128, "all")
        ones_row, B_ones = MBrow("ones_row", 128, "all")
        pats, B_pats = MBr("pats", 4 * 128, "c+p0")
        wcmpr, B_wcmpr = MBr("wcmpr", 2048, "c+p0")
        lmask, B_lmask = MBr("lmask", TPC * 256, "c+p0+p2")
        ck_fm, B_ckfm = MBr("ck_fm", 2048, "c+p0+p2")
        cv_tm, B_cvtm = MBr("cv_tm", 2048, "c+p0+p2")
        GU_lo = AB.alloc("GU", 44 * 1024 * 2, "p4a+p4b")

        def GU(mm_):
            lo = GU_lo + 2 * KB * mm_
            return megaB_h[:, lo // 2: lo // 2 + 1024], ("megaB", lo, lo + 2 * KB)

        wlsc, B_wlsc = MBr("wlsc", 2048, "p0+p3")

        ident_f, B_identf = MFr("ident_f", 128, "all")
        knrep, B_knrep = MFr("knrep", 128, "c+p0+p2")
        qnrep, B_qnrep = MFr("qnrep", 128, "c+p0+p2")
        tabO, B_tabO = MFr("tabO", 2 * NSLOT * 32, "c+p0+p2")
        tabB, B_tabB = MFr("tabB", 2 * 16 * 32, "c+p0")

        ENT, B_ENTx = MBr("ENT", 4 * 2048, "p0")
        efm, B_efm = MBr("efm", 16 * 128, "p0")
        wkv, B_wkv = MBr("wkv", 16 * 256, "c+p0")
        lnst, B_lnst = MBr("lnst", 2048, "p0")
        lnsto, B_lnsto = MBr("lnsto", 2048, "p0")
        lnf_wr, B_lnfw = MBr("lnf_wr", 2048, "p0")
        lnf_stg = [MBr(f"lnf_stg{i}", 2048, "p2") for i in range(2)]

        xst = [MFr(f"xst{i}", 2048, "p0+p3") for i in range(2)]
        angT, B_angT = MFr("angT", 2048, "c", part=32)
        posBs, B_posBs = MFr("posBs", 2048, "c", part=32)
        r_wcmp32, B_rw32 = MFrow("r_wcmp32", 2048, "c")
        r_wcmpb, B_rwb = MBrow("r_wcmpb", 2048, "c")
        r_bkv, B_rbkv = MFrow("r_bkv", 256, "c+p0")
        r_bkv_hi, B_rbkvh = MBrow("r_bkv_hi", 256, "c+p0")
        r_bkv_lo, B_rbkvl = MBrow("r_bkv_lo", 256, "c+p0")
        r_kn, B_rkn = MFrow("r_kn", 128, "c")
        r_qn, B_rqn = MFrow("r_qn", 128, "c")
        r_bcmp, B_rbc = MFrow("r_bcmp", 1, "c+p0")
        r_bcmpb, B_rbcb = MBrow("r_bcmpb", 2, "c+p0")

        WQT_lo = AB.alloc("WQT", 16 * 512 * 2, "p2")

        def WQT(k):
            lo = WQT_lo + 1024 * k
            return megaB_h[:, lo // 2: lo // 2 + 512], ("megaB", lo, lo + 1024)

        wq_all = megaB_h[:, WQT_lo // 2: WQT_lo // 2 + 16 * 512]
        B_wq_all = ("megaB", WQT_lo, WQT_lo + 16 * 1024)
        qtm, B_qtm = MFr("qtm", 1024, "p2")
        qbf, B_qbf = MBr("qbf", 1024, "p2")
        brow_f, B_browf = MFrow("brow_f", 512, "p2")
        brow_hi, B_browhi = MBrow("brow_hi", 512, "p2")
        brow_lo, B_browlo = MBrow("brow_lo", 512, "p2")
        QG_lo = AB.alloc("QG", TPC * HG * 128 * 2, "p2")

        def QG(hh, t):
            lo = QG_lo + (hh * 8 + t) * 256
            return megaB_h[:, lo // 2: lo // 2 + 128], ("megaB", lo, lo + 256)

        LKG_lo = AB.alloc("LKG", NSLOT * HG * 128 * 2, "p2")

        def LKG(hh, sl):
            lo = LKG_lo + (hh * NSLOT + sl) * 256
            return megaB_h[:, lo // 2: lo // 2 + 128], ("megaB", lo, lo + 256)

        def LKGW(hh, sl):
            lo = LKG_lo + (hh * NSLOT + sl) * 256
            return megaB_h[:, lo // 2: lo // 2 + 256], ("megaB", lo, lo + 512)

        LVG_lo = AB.alloc("LVG", NSLOT * HG * 128 * 2, "p2")

        def LVG(sl, hh):
            lo = LVG_lo + (sl * HG + hh) * 256
            return megaB_h[:, lo // 2: lo // 2 + 128], ("megaB", lo, lo + 256)

        MSK_lo = AB.alloc("MSK", sum(TILE_W) * 2, "p2")
        msk_off = []
        _acc = 0
        for t in range(TPC):
            msk_off.append(_acc)
            _acc += TILE_W[t] * 2

        def MSK(t):
            lo = MSK_lo + msk_off[t]
            return megaB_h[:, lo // 2: lo // 2 + TILE_W[t]], \
                ("megaB", lo, lo + 2 * TILE_W[t])

        Pbuf, B_Pbuf = MBr("Pbuf", 2048, "p2")
        PT_lo = AB.alloc("PT", 2 * 2048 * 2, "p2")

        def PTv(par, cols):
            lo = PT_lo + par * 4 * KB
            return megaB_h[:, lo // 2: lo // 2 + cols], ("megaB", lo, lo + 2 * cols)

        plsb, B_plsb = MBr("plsb", 256, "p2")
        pltsb, B_pltsb = MBr("pltsb", 512, "p2")
        mrgb, B_mrgb = MBr("mrgb", 1024, "p2")
        S_sb, B_Ssb = MFr("S_sb", 2048, "p2")
        iota_msk, B_iotam = MFr("iota_msk", 2048, "p2")

        WO_lo = AB.alloc("WO", 16 * 2048 * 2, "p3")

        def WOc(k):
            lo = WO_lo + 4 * KB * k
            return megaB_h[:, lo // 2: lo // 2 + 2048], ("megaB", lo, lo + 4 * KB)

        wo_all = megaB_h[:, WO_lo // 2: WO_lo // 2 + 16 * 2048]
        B_WO_ALL = ("megaB", WO_lo, WO_lo + 64 * KB)
        mstg = [MBr(f"mstg{i}", 2048, "p3") for i in range(2)]
        ln2st, B_ln2st = MBr("ln2st", 2048, "p3")
        bo_f, B_bof = MFrow("bo_f", 2048, "p3")
        bo_hi, B_bohi = MBrow("bo_hi", 2048, "p3")
        bo_lo, B_bolo = MBrow("bo_lo", 2048, "p3")
        borep, B_borep = MFr("borep", 2048, "p3")
        LN2_lo = AB.alloc("LN2", 16 * 1024 * 2, "p3+p4a")

        def LN2c(k):
            lo = LN2_lo + 2 * KB * k
            return megaB_h[:, lo // 2: lo // 2 + 1024], ("megaB", lo, lo + 2 * KB)

        wgbuf = [MBr(f"wg{i}", 2048, "p4a") for i in range(2)]
        wdbuf = [MBr(f"wd{i}", 5632, "p4b") for i in range(2)]
        silu_g, B_silug = MBr("silu_g", 1024, "p4a")
        cs_f, B_csf = MFrow("cs_f", 1024, "p4a+p4b")
        cs_f2, B_csf2 = MFrow("cs_f2", 1024, "p4a+p4b")
        cs_hi, B_cshi = MBrow("cs_hi", 1024, "p4a+p4b")
        cs_lo, B_cslo = MBrow("cs_lo", 1024, "p4a+p4b")
        dstg, B_dstg = MFr("dstg", 1024, "p4b")
        ytile, B_yt = MFr("ytile", 1024, "p4b")

        bn2, B_bn2 = SC(2)
        lncols, B_lncols = SC(4)
        wcc, B_wcc = SC(8)
        cmax_sb, B_cmax = SC(TPC)
        bcmpc, B_bcmpc = SC(1)
        idsc, B_idsc = SC(128)
        ckt, B_ckt = SC(256)
        rmsc, B_rmsc = SC(2)
        ropet, B_ropet = SC(64)
        wrap0, B_wrap0 = SC(512)
        wrap1, B_wrap1 = SC(512)
        tv8, B_tv8 = SC(8)
        st8, B_st8 = SC(8)
        rms4, B_rms4 = SC(4)
        slsb, B_slsb = SC(256)
        sgup, B_sgup = SC(88)
        sdown, B_sdown = SC(16)
        pio, B_pio = SC(128)
        pt1, B_pt1 = SC(128)
        pt2c, B_pt2c = SC(128)
        wdiag = [MBr("wdiagA", 128, "p0"), MBr("wdiagB", 128, "p0")]
        iv32, B_iv32 = SC(1)
        iv32 = small_h[0:32, B_iv32[1] // 4: B_iv32[1] // 4 + 1]

        def IOTA(out_ap, wb, pattern, base, cm):
            OP("gpsimd", lambda e: e.iota(out_ap, pattern=pattern, base=base,
                                          channel_multiplier=cm,
                                          allow_small_or_imprecise_dtypes=True),
               writes=[wb], tag="iota")

        IOTA(idsc, B_idsc, [[-1, 128]], 0, 1)
        OP("vector", lambda e: e.tensor_scalar(ident_bf, idsc, 0.0, None, ALU.is_equal),
           reads=[B_idsc], writes=[B_identb])
        OP("vector", lambda e: e.tensor_scalar(ident_f, idsc, 0.0, None, ALU.is_equal),
           reads=[B_idsc], writes=[B_identf])
        OP("vector", lambda e: e.memset(ones_row, 1.0), writes=[B_ones])
        OP("vector", lambda e: e.memset(ones128, 1.0), writes=[B_ones128])
        IOTA(pio, B_pio, [[-4, 128]], 0, 1)
        for j in range(4):
            pat = pats[:, 128 * j:128 * (j + 1)]
            OP("vector", lambda e, j=j: e.tensor_scalar(pt1, pio, float(-128 * j),
                                                        None, ALU.is_ge),
               reads=[B_pio], writes=[B_pt1], tag="pat")
            OP("vector", lambda e, j=j: e.tensor_scalar(pt2c, pio, float(-128 * j + 3),
                                                        None, ALU.is_le),
               reads=[B_pio], writes=[B_pt2c], tag="pat")
            OP("vector", lambda e, pat=pat: e.tensor_mul(pat, pt1, pt2c),
               reads=[B_pt1, B_pt2c], writes=[B_pats], tag="pat")

        OP("sync", lambda e: e.dma_start(out=r_wcmp32,
                                         in_=d["Wcmp"].rearrange("h one -> one h")),
           writes=[B_rw32], dma_slot="ld0")
        OP("vector", lambda e: e.tensor_copy(r_wcmpb, r_wcmp32),
           reads=[B_rw32], writes=[B_rwb])
        pW, B_pW = PF(0, 2048)
        for q in range(4):
            OP("tensor", lambda e, q=q: e.matmul(pW[:, 512 * q:512 * (q + 1)], ones_row,
                                                 r_wcmpb[:, 512 * q:512 * (q + 1)],
                                                 start=True, stop=True),
               reads=[B_ones, B_rwb], writes=[B_pW])
        OP("vector", lambda e: e.tensor_copy(wcmpr, pW), reads=[B_pW], writes=[B_wcmpr])
        OP("sync", lambda e: e.dma_start(out=r_bcmp, in_=d["bcmp"][None, :]),
           writes=[B_rbc], dma_slot="ld0")
        OP("vector", lambda e: e.tensor_copy(r_bcmpb[:, 0:1], r_bcmp),
           reads=[B_rbc], writes=[B_rbcb])
        pB1, B_pB1 = PF(2048, 64)
        OP("tensor", lambda e: e.matmul(pB1[:, 0:1], ones_row, r_bcmpb[:, 0:1],
                                        start=True, stop=True),
           reads=[B_ones, B_rbcb], writes=[B_pB1])
        OP("vector", lambda e: e.tensor_copy(bcmpc, pB1[:, 0:1]),
           reads=[B_pB1], writes=[B_bcmpc])

        def hi_lo_rows(f_ap, B_f, hi_ap, B_hi, lo_ap, B_lo, n):
            OP("vector", lambda e: e.tensor_copy(hi_ap, f_ap), reads=[B_f], writes=[B_hi])
            OP("vector", lambda e: e.tensor_sub(f_ap, f_ap, hi_ap),
               reads=[B_hi], writes=[B_f])
            OP("vector", lambda e: e.tensor_copy(lo_ap, f_ap), reads=[B_f], writes=[B_lo])

        def replicate_f32(dst_ap, B_dst, hi_ap, B_hi, lo_ap, B_lo, cols, pcol,
                          scale=None):
            pp, B_pp = PF(pcol, cols)
            for q in range((cols + 511) // 512):
                sl = slice(512 * q, min(cols, 512 * (q + 1)))
                OP("tensor", lambda e, sl=sl: e.matmul(pp[:, sl], ones_row, hi_ap[:, sl],
                                                       start=True, stop=False),
                   reads=[B_ones, B_hi], writes=[B_pp])
                OP("tensor", lambda e, sl=sl: e.matmul(pp[:, sl], ones_row, lo_ap[:, sl],
                                                       start=False, stop=True),
                   reads=[B_ones, B_lo], writes=[B_pp])
            if scale is None:
                OP("vector", lambda e: e.tensor_copy(dst_ap, pp), reads=[B_pp],
                   writes=[B_dst])
            else:
                OP("vector", lambda e: e.tensor_scalar_mul(dst_ap, pp, scale),
                   reads=[B_pp], writes=[B_dst])

        OP("sync", lambda e: e.dma_start(out=r_kn, in_=d["kn_w"][None, :]),
           writes=[B_rkn], dma_slot="ld0")
        OP("sync", lambda e: e.dma_start(out=r_qn, in_=d["qn_w"][None, :]),
           writes=[B_rqn], dma_slot="ld0")
        knh, B_knh = MBrow("knh", 128, "c")
        knl, B_knl = MBrow("knl", 128, "c")
        qnh, B_qnh = MBrow("qnh", 128, "c")
        qnl, B_qnl = MBrow("qnl", 128, "c")
        hi_lo_rows(r_kn, B_rkn, knh, B_knh, knl, B_knl, 128)
        hi_lo_rows(r_qn, B_rqn, qnh, B_qnh, qnl, B_qnl, 128)
        replicate_f32(knrep, B_knrep, knh, B_knh, knl, B_knl, 128, 2112)
        replicate_f32(qnrep, B_qnrep, qnh, B_qnh, qnl, B_qnl, 128, 2112, scale=SCALE)

        OP("sync", lambda e: e.dma_start(out=r_bkv[:, 0:128], in_=d["bk"][None, :]),
           writes=[B_rbkv], dma_slot="ld0")
        OP("sync", lambda e: e.dma_start(out=r_bkv[:, 128:256], in_=d["bv"][None, :]),
           writes=[B_rbkv], dma_slot="ld0")
        hi_lo_rows(r_bkv, B_rbkv, r_bkv_hi, B_rbkvh, r_bkv_lo, B_rbkvl, 256)

        OP("sync", lambda e: e.dma_start(out=cmax_sb, in_=d["cmax"][:, :]),
           writes=[B_cmax], dma_slot="ld0")
        OP("gpsimd", lambda e: e.dma_start(out=lmask, in_=d["lmask"][:, :]),
           writes=[B_lmask], dma_slot="ldg")
        OP("sync", lambda e: e.dma_start(out=sgup[:, 0:44],
                                         in_=d["s_gate"].rearrange("(m p) -> p m", p=128),
                                         allow_slow_non_contiguous=True),
           writes=[B_sgup], dma_slot="ld0")
        OP("sync", lambda e: e.dma_start(out=sgup[:, 44:88],
                                         in_=d["s_up"].rearrange("(m p) -> p m", p=128),
                                         allow_slow_non_contiguous=True),
           writes=[B_sgup], dma_slot="ld0")
        OP("sync", lambda e: e.dma_start(out=sdown,
                                         in_=d["s_down"].rearrange("(m p) -> p m", p=128),
                                         allow_slow_non_contiguous=True),
           writes=[B_sdown], dma_slot="ld0")
        OP("gpsimd", lambda e: e.dma_start(
            out=wkv.rearrange("p (k c) -> p k c", k=16)[:, :, 0:128],
            in_=d["Wk"].rearrange("(k p) c -> p k c", p=128)),
           writes=[B_wkv], dma_slot="ldg")
        OP("gpsimd", lambda e: e.dma_start(
            out=wkv.rearrange("p (k c) -> p k c", k=16)[:, :, 128:256],
            in_=d["Wv"].rearrange("(k p) c -> p k c", p=128)),
           writes=[B_wkv], dma_slot="ldg")

        IOTA(iv32, B_iv32, [[0, 1]], 0, 1)
        OP("scalar", lambda e: e.activation(iv32, iv32, AF.Exp, scale=-LN1E4_32),
           reads=[B_iv32], writes=[B_iv32])
        ptr, B_ptr = PF(2048, 512)

        def build_tables(pos_ap, B_pos, kw_ap, B_kw, n, dst_ap, B_dst):
            OP("vector", lambda e: e.tensor_scalar(angT[:, 0:n * 128], pos_ap, iv32,
                                                   None, ALU.mult),
               reads=[B_pos, B_iv32], writes=[B_angT])
            OP("vector", lambda e: e.scalar_tensor_tensor(
                out=pos_ap, in0=kw_ap, scalar=-TWO_PI, in1=angT[:, 0:n * 128],
                op0=ALU.mult, op1=ALU.add),
               reads=[B_kw, B_angT], writes=[B_pos])
            done = 0
            while done < n:
                grp = min(16, n - done)
                g = 32 * grp
                for j in range(grp):
                    OP("tensor", lambda e, j=j, dn=done: e.transpose(
                        ptr[:, 32 * j:32 * (j + 1)],
                        pos_ap[:, 128 * (dn + j):128 * (dn + j + 1)],
                        ident_f[0:32, 0:32]),
                       reads=[B_pos, B_identf], writes=[B_ptr], tag="ttab")
                co = dst_ap[:, 32 * done:32 * done + g]
                so = dst_ap[:, n * 32 + 32 * done:n * 32 + 32 * done + g]
                for shift, outa in ((0.0, so), (PI / 2, co)):
                    if shift == 0.0:
                        OP("vector", lambda e, g=g: e.tensor_copy(wrap0[:, 0:g],
                                                                  ptr[:, 0:g]),
                           reads=[B_ptr], writes=[B_wrap0])
                    else:
                        OP("vector", lambda e, g=g, sh=shift: e.tensor_scalar_add(
                            wrap0[:, 0:g], ptr[:, 0:g], sh),
                           reads=[B_ptr], writes=[B_wrap0])
                    for cmp_op, thr, delta in ((ALU.is_gt, PI, -TWO_PI),
                                               (ALU.is_lt, -PI, TWO_PI)):
                        OP("vector", lambda e, g=g, co_=cmp_op, th=thr, dl=delta:
                           e.tensor_scalar(wrap1[:, 0:g], wrap0[:, 0:g], th, dl,
                                           co_, ALU.mult),
                           reads=[B_wrap0], writes=[B_wrap1])
                        OP("vector", lambda e, g=g: e.tensor_add(
                            wrap0[:, 0:g], wrap0[:, 0:g], wrap1[:, 0:g]),
                           reads=[B_wrap0, B_wrap1], writes=[B_wrap0])
                    OP("vector", lambda e, g=g: e.tensor_scalar_min(wrap0[:, 0:g],
                                                                    wrap0[:, 0:g], PI),
                       reads=[B_wrap0], writes=[B_wrap0])
                    OP("vector", lambda e, g=g: e.tensor_scalar_max(wrap0[:, 0:g],
                                                                    wrap0[:, 0:g], -PI),
                       reads=[B_wrap0], writes=[B_wrap0])
                    OP("scalar", lambda e, g=g, oa=outa: e.activation(oa, wrap0[:, 0:g],
                                                                      AF.Sin),
                       reads=[B_wrap0], writes=[B_dst])
                done += grp

        OP("sync", lambda e: e.dma_start(out=posBs, in_=d["posB"][:, :]),
           writes=[B_posBs], dma_slot="ld0")
        kw_in, B_kwin = MFr("kw_in", 2048, "c", part=32)
        OP("sync", lambda e: e.dma_start(out=kw_in, in_=d["kwB"][:, :]),
           writes=[B_kwin], dma_slot="ld0")
        build_tables(posBs, B_posBs, kw_in, B_kwin, 16, tabB, B_tabB)
        OP("sync", lambda e: e.dma_start(out=posBs[:, 0:NSLOT * 128],
                                         in_=d["posT_own"][:, :]),
           writes=[B_posBs], dma_slot="ld0")
        OP("sync", lambda e: e.dma_start(out=kw_in[:, 0:NSLOT * 128],
                                         in_=d["kw_own"][:, :]),
           writes=[B_kwin], dma_slot="ld0")
        build_tables(posBs[:, 0:NSLOT * 128], B_posBs, kw_in[:, 0:NSLOT * 128],
                     B_kwin, NSLOT, tabO, B_tabO)

        shuf1 = [i ^ 1 for i in range(32)]
        shuf2 = [i ^ 2 for i in range(32)]
        ent_ps, B_entps = PF(0, 2048)
        pt_bf, B_ptbf = PB(0, 1024)
        ck_ps, B_ckps = PF(2048, 256)

        def ln_tile(x_ap, B_x, out_ap, B_out, dump, B_dump, tag=""):
            sm = bn2[:, 0:1]
            ss = bn2[:, 1:2]
            inv = lncols[:, 0:1]
            nmi = lncols[:, 1:2]
            tt = lncols[:, 2:3]
            OP("vector", lambda e: e.reduce_sum(sm, x_ap, axis=AX.X),
               reads=[B_x], writes=[B_bn2], tag=tag)
            OP("vector", lambda e: e.memset(ss, 0.0), writes=[B_bn2], tag=tag)
            OP("vector", lambda e: e.scalar_tensor_tensor(
                out=dump, in0=x_ap, scalar=1.0, in1=x_ap,
                op0=ALU.bypass, op1=ALU.mult, accum_out=ss),
               reads=[B_x], writes=[B_dump, B_bn2], tag=tag)
            OP("vector", lambda e: e.tensor_scalar_mul(sm, sm, 1.0 / HID),
               reads=[B_bn2], writes=[B_bn2], tag=tag)
            OP("vector", lambda e: e.tensor_mul(tt, sm, sm),
               reads=[B_bn2], writes=[B_lncols], tag=tag)
            OP("vector", lambda e: e.tensor_scalar_mul(inv, ss, 1.0 / HID),
               reads=[B_bn2], writes=[B_lncols], tag=tag)
            OP("vector", lambda e: e.tensor_sub(inv, inv, tt),
               reads=[B_lncols], writes=[B_lncols], tag=tag)
            OP("vector", lambda e: e.tensor_scalar_add(inv, inv, EPS_LN),
               reads=[B_lncols], writes=[B_lncols], tag=tag)
            OP("scalar", lambda e: e.activation(inv, inv, AF.Sqrt),
               reads=[B_lncols], writes=[B_lncols], tag=tag)
            OP("vector", lambda e: e.reciprocal(inv, inv),
               reads=[B_lncols], writes=[B_lncols], tag=tag)
            OP("vector", lambda e: e.tensor_mul(nmi, sm, inv),
               reads=[B_bn2, B_lncols], writes=[B_lncols], tag=tag)
            OP("vector", lambda e: e.tensor_scalar_mul(nmi, nmi, -1.0),
               reads=[B_lncols], writes=[B_lncols], tag=tag)
            OP("scalar", lambda e: e.activation(out_ap, x_ap, AF.Identity, bias=nmi,
                                                scale=inv),
               reads=[B_x, B_lncols], writes=[B_out], tag=tag)

        def ENTt(i4):
            lo = B_ENTx[1] + 4096 * i4
            return megaB_h[:, lo // 2: lo // 2 + 2048], ("megaB", lo, lo + 4096)

        for gt in range(NT):
            xb, B_xb = xst[gt % 2]
            OP("sync", lambda e, gt=gt, xb=xb: e.dma_start(
                out=xb, in_=d["x"][128 * gt:128 * (gt + 1), :]),
               writes=[B_xb], dma_slot=f"xs{gt % 2}")
            ln_tile(xb, B_xb, lnst, B_lnst, wlsc, B_wlsc, tag="p0ln")
            wcol = wcc[:, 0:1]
            OP("vector", lambda e: e.memset(wcol, 0.0), writes=[B_wcc], tag="p0wz")
            OP("vector", lambda e: e.scalar_tensor_tensor(
                out=wlsc, in0=lnst, scalar=1.0, in1=wcmpr,
                op0=ALU.bypass, op1=ALU.mult, accum_out=wcol),
               reads=[B_lnst, B_wcmpr], writes=[B_wlsc, B_wcc], tag="p0wlog")
            OP("vector", lambda e: e.tensor_add(wcol, wcol, bcmpc),
               reads=[B_bcmpc], writes=[B_wcc])
            m4 = wcc[:, 1:2]; ee = wcc[:, 2:3]; s4 = wcc[:, 3:4]
            rr = wcc[:, 4:5]; t0 = wcc[:, 5:6]; nm4 = wcc[:, 6:7]
            OP("vector", lambda e: e.stream_shuffle(t0, wcol, shuf1), reads=[B_wcc], writes=[B_wcc])
            OP("vector", lambda e: e.tensor_max(m4, wcol, t0), reads=[B_wcc], writes=[B_wcc])
            OP("vector", lambda e: e.stream_shuffle(t0, m4, shuf2), reads=[B_wcc], writes=[B_wcc])
            OP("vector", lambda e: e.tensor_max(m4, m4, t0), reads=[B_wcc], writes=[B_wcc])
            OP("vector", lambda e: e.tensor_scalar_mul(nm4, m4, -1.0), reads=[B_wcc], writes=[B_wcc])
            OP("scalar", lambda e: e.activation(ee, wcol, AF.Exp, bias=nm4),
               reads=[B_wcc], writes=[B_wcc])
            OP("vector", lambda e: e.stream_shuffle(t0, ee, shuf1), reads=[B_wcc], writes=[B_wcc])
            OP("vector", lambda e: e.tensor_add(s4, ee, t0), reads=[B_wcc], writes=[B_wcc])
            OP("vector", lambda e: e.stream_shuffle(t0, s4, shuf2), reads=[B_wcc], writes=[B_wcc])
            OP("vector", lambda e: e.tensor_add(s4, s4, t0), reads=[B_wcc], writes=[B_wcc])
            OP("vector", lambda e: e.reciprocal(rr, s4), reads=[B_wcc], writes=[B_wcc])
            OP("vector", lambda e: e.tensor_mul(ee, ee, rr), reads=[B_wcc], writes=[B_wcc])
            j = gt % 4
            wd_ap, B_wd = wdiag[gt % 2]
            OP("vector", lambda e, j=j, wd_ap=wd_ap: e.tensor_scalar(
                wd_ap, pats[:, 128 * j:128 * (j + 1)], ee, None, ALU.mult),
               reads=[B_pats, B_wcc], writes=[B_wd], tag="p0wd")
            for q in range(4):
                OP("tensor", lambda e, q=q, j=j, wd_ap=wd_ap: e.matmul(
                    ent_ps[:, 512 * q:512 * (q + 1)], wd_ap,
                    lnst[:, 512 * q:512 * (q + 1)], start=(j == 0), stop=(j == 3)),
                   reads=[B_wd, B_lnst], writes=[B_entps], tag="p0mm")
            if j == 3:
                b = gt // 4
                ent_ap, B_ent = ENTt(b % 4)
                OP("vector", lambda e, ent_ap=ent_ap: e.tensor_copy(ent_ap, ent_ps),
                   reads=[B_entps], writes=[B_ent], tag="p0ev")
                for half in range(2):
                    for k8 in range(8):
                        k = 8 * half + k8
                        OP("tensor", lambda e, k=k, k8=k8, ent_ap=ent_ap: e.transpose(
                            pt_bf[:, 128 * k8:128 * (k8 + 1)],
                            ent_ap[:, 128 * k:128 * (k + 1)], ident_bf),
                           reads=[B_ent, B_identb], writes=[B_ptbf], tag="p1tr")
                    for k8 in range(8):
                        k = 8 * half + k8
                        OP("vector", lambda e, k=k, k8=k8: e.tensor_copy(
                            efm[:, 128 * k:128 * (k + 1)],
                            pt_bf[:, 128 * k8:128 * (k8 + 1)]),
                           reads=[B_ptbf], writes=[B_efm], tag="p1trev")
                OP("tensor", lambda e: e.matmul(ck_ps, ones_row, r_bkv_hi,
                                                start=True, stop=False),
                   reads=[B_ones, B_rbkvh], writes=[B_ckps])
                OP("tensor", lambda e: e.matmul(ck_ps, ones_row, r_bkv_lo,
                                                start=False, stop=False),
                   reads=[B_ones, B_rbkvl], writes=[B_ckps])
                wkv3 = wkv.rearrange("p (kk c) -> p kk c", kk=16)
                for k in range(16):
                    OP("tensor", lambda e, k=k: e.matmul(
                        ck_ps, efm[:, 128 * k:128 * (k + 1)], wkv3[:, k, :],
                        start=False, stop=(k == 15)),
                       reads=[B_efm, B_wkv], writes=[B_ckps], tag="p1mm")
                OP("vector", lambda e: e.tensor_copy(ckt, ck_ps), reads=[B_ckps],
                   writes=[B_ckt])
                OP("vector", lambda e, b=b: e.tensor_copy(
                    cv_tm[:, 128 * b:128 * (b + 1)], ckt[:, 128:256]),
                   reads=[B_ckt], writes=[B_cvtm])
                ss2 = rmsc[:, 0:1]
                OP("vector", lambda e: e.memset(ss2, 0.0), writes=[B_rmsc])
                OP("scalar", lambda e: e.activation(lnsto[:, 0:128], ckt[:, 0:128],
                                                    AF.Square, accum_out=ss2),
                   reads=[B_ckt], writes=[B_lnsto, B_rmsc], tag="p1rms")
                OP("vector", lambda e: e.tensor_scalar(ss2, ss2, 1.0 / 128, EPS_RMS,
                                                       ALU.mult, ALU.add),
                   reads=[B_rmsc], writes=[B_rmsc])
                OP("scalar", lambda e: e.activation(ss2, ss2, AF.Sqrt),
                   reads=[B_rmsc], writes=[B_rmsc])
                OP("vector", lambda e: e.reciprocal(ss2, ss2), reads=[B_rmsc],
                   writes=[B_rmsc])
                OP("vector", lambda e: e.tensor_scalar(ckt[:, 0:128], ckt[:, 0:128],
                                                       ss2, None, ALU.mult),
                   reads=[B_rmsc], writes=[B_ckt])
                OP("vector", lambda e: e.tensor_mul(ckt[:, 0:128], ckt[:, 0:128], knrep),
                   reads=[B_knrep], writes=[B_ckt])
                cosb = tabB[:, 32 * b:32 * (b + 1)]
                sinb = tabB[:, 512 + 32 * b:512 + 32 * (b + 1)]
                r0 = ckt[:, 0:32]; r1 = ckt[:, 32:64]
                OP("vector", lambda e, cosb=cosb: e.tensor_mul(ropet[:, 0:32], r0, cosb),
                   reads=[B_ckt, B_tabB], writes=[B_ropet])
                OP("vector", lambda e, sinb=sinb: e.tensor_mul(ropet[:, 32:64], r1, sinb),
                   reads=[B_ckt, B_tabB], writes=[B_ropet])
                OP("vector", lambda e, cosb=cosb: e.tensor_mul(r1, r1, cosb),
                   reads=[B_tabB], writes=[B_ckt])
                OP("vector", lambda e, sinb=sinb: e.tensor_mul(r0, r0, sinb),
                   reads=[B_tabB], writes=[B_ckt])
                OP("vector", lambda e: e.tensor_add(ckt[:, 32:64], r0, r1),
                   reads=[B_ckt], writes=[B_ckt])
                OP("vector", lambda e: e.tensor_sub(ckt[:, 0:32], ropet[:, 0:32],
                                                    ropet[:, 32:64]),
                   reads=[B_ropet], writes=[B_ckt])
                OP("vector", lambda e: e.tensor_copy(lnsto[:, 0:128], ckt[:, 0:128]),
                   reads=[B_ckt], writes=[B_lnsto])
                OP("tensor", lambda e: e.transpose(pt_bf[:, 0:128], lnsto[:, 0:128],
                                                   ident_bf),
                   reads=[B_lnsto, B_identb], writes=[B_ptbf])
                OP("vector", lambda e, b=b: e.tensor_copy(
                    ck_fm[:, 128 * b:128 * (b + 1)], pt_bf[:, 0:128]),
                   reads=[B_ptbf], writes=[B_ckfm])

        for sl in range(NSLOT):
            xb, B_xb = xst[sl % 2]
            OP("sync", lambda e, sl=sl, xb=xb: e.dma_start(
                out=xb, in_=d["x_own"][128 * sl:128 * (sl + 1), :]),
               writes=[B_xb], dma_slot=f"xs{sl % 2}")
            ln_tile(xb, B_xb, lnsto, B_lnsto, wlsc, B_wlsc, tag="p0own")
            for half in range(2):
                for k8 in range(8):
                    k = 8 * half + k8
                    OP("tensor", lambda e, k=k, k8=k8: e.transpose(
                        pt_bf[:, 128 * k8:128 * (k8 + 1)],
                        lnsto[:, 128 * k:128 * (k + 1)], ident_bf),
                       reads=[B_lnsto, B_identb], writes=[B_ptbf], tag="p0tr")
                for k8 in range(8):
                    k = 8 * half + k8
                    OP("vector", lambda e, k=k, k8=k8: e.tensor_copy(
                        lnf_wr[:, 128 * k:128 * (k + 1)],
                        pt_bf[:, 128 * k8:128 * (k8 + 1)]),
                       reads=[B_ptbf],
                       writes=[(B_lnfw[0], B_lnfw[1] + 256 * k,
                                B_lnfw[1] + 256 * (k + 1))], tag="p0trev")
            OP("sync", lambda e, sl=sl: e.dma_start(out=lnf_hbm[sl, :, :], in_=lnf_wr),
               reads=[B_lnfw],
               writes=[("dram_lnf", 4096 * sl, 4096 * (sl + 1))],
               dma_slot="lnw", tag="lnfw")

        ps_proj = [PF(0, 512), PF(512, 512)]
        pt2_bf = [PB(0, 1024), PB(1024, 1024)]
        ps_S = [PF(1024, 512), PF(1536, 512)]
        ps_mrg, B_psmrg = PF(0, 1024)
        ps_sl, B_pssl = PF(2048, 256)
        ps_PT = [PB(0, 1024), PB(1024, 1024)]
        ps_PLT, B_psPLT = PB(2048, 512)

        IOTA(iota_msk, B_iotam, [[1, 2048]], 0, 0)
        for t in range(TPC):
            m_ap, B_m = MSK(t)
            OP("vector", lambda e, m_ap=m_ap, t=t, W=TILE_W[t]: e.tensor_scalar(
                m_ap, iota_msk[:, 0:W], cmax_sb[:, t:t + 1], None, ALU.is_gt),
               reads=[B_iotam, B_cmax], writes=[B_m], tag="mskb")
            OP("vector", lambda e, m_ap=m_ap: e.tensor_scalar_mul(m_ap, m_ap, NEG),
               reads=[B_m], writes=[B_m], tag="mskb")

        def proj_load_bias(dsrc, g):
            OP("sync", lambda e: e.dma_start(out=brow_f,
                                             in_=dsrc[None, 512 * g:512 * (g + 1)]),
               writes=[B_browf], dma_slot="ld0")
            hi_lo_rows(brow_f, B_browf, brow_hi, B_browhi, brow_lo, B_browlo, 512)

        def proj_dma_W(dsrc, g, slot):
            OP("gpsimd", lambda e, g=g, dsrc=dsrc: e.dma_start(
                out=wq_all.rearrange("p (k c) -> p k c", k=16),
                in_=dsrc[:, 512 * g:512 * (g + 1)].rearrange("(k p) c -> p k c", p=128)),
               writes=[B_wq_all], dma_slot=slot, tag="wdma")

        def rms_rope_store(sl, q_lo, is_q, dst_fn):
            qv = qtm[:, q_lo:q_lo + 512]
            v3 = qv.rearrange("p (h dd) -> p h dd", h=4)
            OP("vector", lambda e: e.memset(rms4, 0.0), writes=[B_rms4])
            OP("vector", lambda e, qv=qv: e.scalar_tensor_tensor(
                out=qbf[:, 512:1024], in0=qv, scalar=1.0, in1=qv,
                op0=ALU.bypass, op1=ALU.mult), reads=[B_qtm], writes=[B_qbf], tag="rr")
            OP("vector", lambda e: e.reduce_sum(
                rms4, qbf[:, 512:1024].rearrange("p (h dd) -> p h dd", h=4), axis=AX.X),
               reads=[B_qbf], writes=[B_rms4], tag="rr")
            OP("vector", lambda e: e.tensor_scalar_mul(rms4, rms4, 1.0 / 128),
               reads=[B_rms4], writes=[B_rms4])
            OP("vector", lambda e: e.tensor_scalar_add(rms4, rms4, EPS_RMS),
               reads=[B_rms4], writes=[B_rms4])
            OP("scalar", lambda e: e.activation(rms4, rms4, AF.Sqrt),
               reads=[B_rms4], writes=[B_rms4])
            OP("vector", lambda e: e.reciprocal(rms4, rms4),
               reads=[B_rms4], writes=[B_rms4])
            for hh in range(4):
                OP("vector", lambda e, v3=v3, hh=hh: e.tensor_scalar(
                    v3[:, hh, :], v3[:, hh, :], rms4[:, hh:hh + 1], None, ALU.mult),
                   reads=[B_rms4, B_qtm], writes=[B_qtm], tag="rr")
            gain = qnrep if is_q else knrep
            B_gain = B_qnrep if is_q else B_knrep
            OP("vector", lambda e, v3=v3, gain=gain: e.tensor_mul(
                v3, v3, gain.unsqueeze(1).to_broadcast([128, 4, 128])),
               reads=[B_qtm, B_gain], writes=[B_qtm], tag="rr")
            cosA = tabO[:, 32 * sl:32 * (sl + 1)]
            sinA = tabO[:, NSLOT * 32 + 32 * sl:NSLOT * 32 + 32 * (sl + 1)]
            cb = lambda t_: t_.unsqueeze(1).to_broadcast([128, 4, 32])
            r0 = v3[:, :, 0:32]; r1 = v3[:, :, 32:64]
            t0v = slsb[:, 0:128].rearrange("p (h dd) -> p h dd", h=4)
            t1v = slsb[:, 128:256].rearrange("p (h dd) -> p h dd", h=4)
            OP("vector", lambda e, r0=r0, cosA=cosA, t0v=t0v: e.tensor_mul(
                t0v, r0, cb(cosA)), reads=[B_qtm, B_tabO], writes=[B_slsb], tag="rr")
            OP("vector", lambda e, r1=r1, sinA=sinA, t1v=t1v: e.tensor_mul(
                t1v, r1, cb(sinA)), reads=[B_qtm, B_tabO], writes=[B_slsb], tag="rr")
            OP("vector", lambda e, r1=r1, cosA=cosA: e.tensor_mul(r1, r1, cb(cosA)),
               reads=[B_tabO, B_qtm], writes=[B_qtm], tag="rr")
            OP("vector", lambda e, r0=r0, sinA=sinA: e.tensor_mul(r0, r0, cb(sinA)),
               reads=[B_tabO, B_qtm], writes=[B_qtm], tag="rr")
            OP("vector", lambda e, v3=v3, r0=r0, r1=r1: e.tensor_add(
                v3[:, :, 32:64], r0, r1), reads=[B_qtm], writes=[B_qtm], tag="rr")
            OP("vector", lambda e, v3=v3, t0v=t0v, t1v=t1v: e.tensor_sub(
                v3[:, :, 0:32], t0v, t1v), reads=[B_slsb, B_qtm], writes=[B_qtm],
               tag="rr")
            OP("vector", lambda e, qv=qv: e.tensor_copy(qbf[:, 0:512], qv),
               reads=[B_qtm], writes=[B_qbf], tag="rr")
            par = sl % 2
            pt_ap, B_pt = pt2_bf[par]
            for hh in range(4):
                OP("tensor", lambda e, hh=hh, pt_ap=pt_ap: e.transpose(
                    pt_ap[:, 128 * hh:128 * (hh + 1)],
                    qbf[:, 128 * hh:128 * (hh + 1)], ident_bf),
                   reads=[B_qbf, B_identb], writes=[B_pt], tag="rrT")
            for hh in range(4):
                dap, B_d2 = dst_fn(hh)
                OP("vector", lambda e, dap=dap, hh=hh, pt_ap=pt_ap: e.tensor_copy(
                    dap, pt_ap[:, 128 * hh:128 * (hh + 1)]),
                   reads=[B_pt], writes=[B_d2], tag="rrTe")

        def lv_store(sl, q_lo):
            qv = qtm[:, q_lo:q_lo + 512]
            for hh in range(4):
                dap, B_d2 = LVG(sl, hh)
                OP("vector", lambda e, dap=dap, qv=qv, hh=hh: e.tensor_copy(
                    dap, qv[:, 128 * hh:128 * (hh + 1)]),
                   reads=[B_qtm], writes=[B_d2], tag="lve")

        def proj_group(dW, dB, g, slots, kind, dma_slot):
            proj_dma_W(dW, g, dma_slot)
            proj_load_bias(dB, g)
            for i, sl in enumerate(slots):
                par = i % 2
                pp, B_pp = ps_proj[par]
                stg_ap, B_stg = lnf_stg[par]
                OP("sync", lambda e, sl=sl, stg_ap=stg_ap: e.dma_start(
                    out=stg_ap, in_=lnf_hbm[sl, :, :]),
                   reads=[("dram_lnf", 4096 * sl, 4096 * (sl + 1))],
                   writes=[B_stg], dma_slot=f"lf{par}", tag="lnfr")
                OP("tensor", lambda e, pp=pp: e.matmul(pp, ones_row, brow_hi,
                                                       start=True, stop=False),
                   reads=[B_ones, B_browhi], writes=[B_pp], tag="pj")
                OP("tensor", lambda e, pp=pp: e.matmul(pp, ones_row, brow_lo,
                                                       start=False, stop=False),
                   reads=[B_ones, B_browlo], writes=[B_pp], tag="pj")
                for k in range(16):
                    wap, B_w = WQT(k)
                    OP("tensor", lambda e, pp=pp, stg_ap=stg_ap, wap=wap,
                       k=k: e.matmul(
                        pp, stg_ap[:, 128 * k:128 * (k + 1)], wap,
                        start=False, stop=(k == 15)),
                       reads=[B_stg, B_w], writes=[B_pp], tag="pjmm")
                q_lo = 512 * par
                OP("vector", lambda e, pp=pp, q_lo=q_lo: e.tensor_copy(
                    qtm[:, q_lo:q_lo + 512], pp), reads=[B_pp], writes=[B_qtm],
                   tag="pjev")
                if kind == "q":
                    t_own = i
                    rms_rope_store(sl, q_lo, True,
                                   lambda hh, t_own=t_own: QG(hh, t_own))
                elif kind == "lk":
                    rms_rope_store(sl, q_lo, False, lambda hh, sl=sl: LKG(hh, sl))
                else:
                    lv_store(sl, q_lo)

        def attn_head(g, hh):
            h = HG * g + hh
            for pp_i in range(NPAIR):
                W = PAIR_W[pp_i]
                for ti in range(2):
                    t = 2 * pp_i + ti
                    sl_cur = 3 * pp_i + 1 + ti
                    nch = W // 512
                    qg_ap, B_qg = QG(hh, t)
                    for cc in range(nch):
                        sp, B_sp = ps_S[cc % 2]
                        OP("tensor", lambda e, sp=sp, qg_ap=qg_ap, cc=cc: e.matmul(
                            sp, qg_ap, ck_fm[:, 512 * cc:512 * (cc + 1)],
                            start=True, stop=True),
                           reads=[B_qg, B_ckfm], writes=[B_sp], tag="sc")
                        m_ap, B_m = MSK(t)
                        OP("vector", lambda e, sp=sp, m_ap=m_ap, cc=cc: e.tensor_add(
                            S_sb[:, 512 * cc:512 * (cc + 1)], sp,
                            m_ap[:, 512 * cc:512 * (cc + 1)]),
                           reads=[B_sp, B_m], writes=[B_Ssb], tag="scev")
                    OP("vector", lambda e, W=W: e.max(out=tv8, in_=S_sb[:, 0:W]),
                       reads=[B_Ssb], writes=[B_tv8], tag="max8")
                    negm = st8[:, 0:1]; t2 = st8[:, 1:2]; Zc = st8[:, 2:3]
                    rZ = st8[:, 3:4]
                    OP("vector", lambda e: e.tensor_scalar_mul(negm, tv8[:, 0:1], -1.0),
                       reads=[B_tv8], writes=[B_st8])
                    OP("vector", lambda e: e.tensor_scalar_max(t2, tv8[:, 7:8], -5.0e29),
                       reads=[B_tv8], writes=[B_st8])
                    OP("vector", lambda e: e.memset(Zc, 0.0), writes=[B_st8])
                    OP("scalar", lambda e: e.activation(qbf[:, 512:520], tv8, AF.Exp,
                                                        bias=negm, accum_out=Zc),
                       reads=[B_tv8, B_st8], writes=[B_qbf, B_st8], tag="exp8")
                    OP("vector", lambda e: e.reciprocal(rZ, Zc),
                       reads=[B_st8], writes=[B_st8])
                    OP("scalar", lambda e, W=W: e.activation(
                        Pbuf[:, 0:W], S_sb[:, 0:W], AF.Exp, bias=negm),
                       reads=[B_Ssb, B_st8], writes=[B_Pbuf], tag="expP")
                    OP("vector", lambda e, W=W: e.scalar_tensor_tensor(
                        out=Pbuf[:, 0:W], in0=S_sb[:, 0:W], scalar=t2,
                        in1=Pbuf[:, 0:W], op0=ALU.is_ge, op1=ALU.mult),
                       reads=[B_Ssb, B_st8, B_Pbuf], writes=[B_Pbuf], tag="maskP")
                    OP("vector", lambda e, W=W: e.tensor_scalar(
                        Pbuf[:, 0:W], Pbuf[:, 0:W], rZ, None, ALU.mult),
                       reads=[B_st8, B_Pbuf], writes=[B_Pbuf], tag="scaleP")
                    nkc = W // 128
                    pt_dst, B_ptd = PTv(ti, W)
                    done = 0
                    while done < nkc:
                        grp = min(8, nkc - done)
                        pta, B_pta = ps_PT[(done // 8) % 2]
                        for jj in range(grp):
                            OP("tensor", lambda e, jj=jj, dn=done, pta=pta: e.transpose(
                                pta[:, 128 * jj:128 * (jj + 1)],
                                Pbuf[:, 128 * (dn + jj):128 * (dn + jj + 1)], ident_bf),
                               reads=[B_Pbuf, B_identb], writes=[B_pta], tag="PT")
                        OP("vector", lambda e, dn=done, grp=grp, pta=pta,
                           pt_dst=pt_dst: e.tensor_copy(
                            pt_dst[:, 128 * dn:128 * (dn + grp)], pta[:, 0:128 * grp]),
                           reads=[B_pta], writes=[B_ptd], tag="PTev")
                        done += grp
                    lkw, B_lkw = LKGW(hh, sl_cur - 1)
                    OP("tensor", lambda e, qg_ap=qg_ap, lkw=lkw: e.matmul(
                        ps_sl, qg_ap, lkw, start=True, stop=True),
                       reads=[B_qg, B_lkw], writes=[B_pssl], tag="sl")
                    OP("vector", lambda e, t=t: e.tensor_add(
                        slsb, ps_sl, lmask[:, 256 * t:256 * (t + 1)]),
                       reads=[B_pssl, B_lmask], writes=[B_slsb], tag="slev")
                    ml = st8[:, 4:5]; Zl = st8[:, 5:6]; rl = st8[:, 6:7]
                    OP("vector", lambda e: e.reduce_max(ml, slsb, axis=AX.X),
                       reads=[B_slsb], writes=[B_st8])
                    OP("vector", lambda e: e.tensor_scalar_mul(ml, ml, -1.0),
                       reads=[B_st8], writes=[B_st8])
                    OP("vector", lambda e: e.memset(Zl, 0.0), writes=[B_st8])
                    OP("scalar", lambda e: e.activation(plsb, slsb, AF.Exp, bias=ml,
                                                        accum_out=Zl),
                       reads=[B_slsb, B_st8], writes=[B_plsb, B_st8], tag="expL")
                    OP("vector", lambda e: e.reciprocal(rl, Zl),
                       reads=[B_st8], writes=[B_st8])
                    OP("vector", lambda e: e.tensor_scalar(plsb, plsb, rl, None,
                                                           ALU.mult),
                       reads=[B_st8, B_plsb], writes=[B_plsb], tag="plm")
                    for jj in range(2):
                        OP("tensor", lambda e, jj=jj: e.transpose(
                            ps_PLT[:, 128 * jj:128 * (jj + 1)],
                            plsb[:, 128 * jj:128 * (jj + 1)], ident_bf),
                           reads=[B_plsb, B_identb], writes=[B_psPLT], tag="PLT")
                    OP("vector", lambda e, ti=ti: e.tensor_copy(
                        pltsb[:, 256 * ti:256 * (ti + 1)], ps_PLT[:, 0:256]),
                       reads=[B_psPLT], writes=[B_pltsb], tag="PLTev")
                nkc = W // 128
                for kc in range(nkc):
                    for ti in range(2):
                        t = 2 * pp_i + ti
                        bank_first = (pp_i % 2 == 0) and (kc == 0) and (ti == 0)
                        pt_dst, B_ptd = PTv(ti, W)
                        OP("tensor", lambda e, kc=kc, t=t, pt_dst=pt_dst,
                           st=bank_first: e.matmul(
                            ps_mrg[:, 128 * t:128 * (t + 1)],
                            cv_tm[:, 128 * kc:128 * (kc + 1)],
                            pt_dst[:, 128 * kc:128 * (kc + 1)],
                            start=st, stop=False, skip_group_check=True),
                           reads=[B_cvtm, B_ptd], writes=[B_psmrg], tag="ctx")
                for ti in range(2):
                    t = 2 * pp_i + ti
                    sl_cur = 3 * pp_i + 1 + ti
                    for jj, sl_kv in enumerate((sl_cur - 1, sl_cur)):
                        bank_last = (pp_i % 2 == 1) and (ti == 1) and (jj == 1)
                        lv_ap, B_lv = LVG(sl_kv, hh)
                        OP("tensor", lambda e, lv_ap=lv_ap, jj=jj, t=t, ti=ti,
                           sp=bank_last: e.matmul(
                            ps_mrg[:, 128 * t:128 * (t + 1)], lv_ap,
                            pltsb[:, 256 * ti + 128 * jj:256 * ti + 128 * (jj + 1)],
                            start=False, stop=sp, skip_group_check=True),
                           reads=[B_lv, B_pltsb], writes=[B_psmrg], tag="lctx")
            OP("vector", lambda e: e.tensor_scalar_mul(mrgb, ps_mrg, 0.5),
               reads=[B_psmrg], writes=[B_mrgb], tag="mgev")
            OP("sync", lambda e, h=h: e.dma_start(out=merged_hbm[h, :, :], in_=mrgb),
               reads=[B_mrgb], writes=[("dram_mrg", h * 1024, (h + 1) * 1024)],
               dma_slot="mrg")

        for g in range(NHG):
            own_slots_q = [3 * (t // 2) + 1 + (t % 2) for t in range(TPC)]
            proj_group(d["Wq"], d["bq"], g, own_slots_q, "q", "wq")
            proj_group(d["Wlk"], d["blk"], g, list(range(NSLOT)), "lk", "wq")
            proj_group(d["Wlv"], d["blv"], g, list(range(NSLOT)), "lv", "wq")
            for hh in range(HG):
                attn_head(g, hh)

        OP("gpsimd", lambda e: e.dma_start(
            out=wo_all.rearrange("p (k c) -> p k c", k=16),
            in_=d["Wo"].rearrange("(k p) c -> p k c", p=128)),
           writes=[B_WO_ALL], dma_slot="wo", tag="wodma")
        OP("sync", lambda e: e.dma_start(out=bo_f, in_=d["bo"][None, :]),
           writes=[B_bof], dma_slot="ld0")
        hi_lo_rows(bo_f, B_bof, bo_hi, B_bohi, bo_lo, B_bolo, 2048)
        pbr, B_pbr = PF(0, 2048)
        for q in range(4):
            OP("tensor", lambda e, q=q: e.matmul(pbr[:, 512 * q:512 * (q + 1)], ones_row,
                                                 bo_hi[:, 512 * q:512 * (q + 1)],
                                                 start=True, stop=False),
               reads=[B_ones, B_bohi], writes=[B_pbr])
            OP("tensor", lambda e, q=q: e.matmul(pbr[:, 512 * q:512 * (q + 1)], ones_row,
                                                 bo_lo[:, 512 * q:512 * (q + 1)],
                                                 start=False, stop=True),
               reads=[B_ones, B_bolo], writes=[B_pbr])
        OP("vector", lambda e: e.tensor_copy(borep, pbr), reads=[B_pbr], writes=[B_borep])

        ps_wo, B_pswo = PF(0, 2048)
        for t in range(TPC):
            ms_ap, B_ms = mstg[t % 2]
            OP("sync", lambda e, t=t, ms_ap=ms_ap: e.dma_start(
                out=ms_ap.rearrange("p (h c) -> p h c", h=16),
                in_=merged_hbm[:, :, 128 * t:128 * (t + 1)].transpose([1, 0, 2])),
               reads=[("dram_mrg", 0, NH * 1024)], writes=[B_ms],
               dma_slot=f"ms{t % 2}", tag="msdma")
            xb, B_xb = xst[0]
            sl_x = 3 * (t // 2) + 1 + (t % 2)
            OP("sync", lambda e, sl_x=sl_x, xb=xb: e.dma_start(
                out=xb, in_=d["x_own"][128 * sl_x:128 * (sl_x + 1), :]),
               writes=[B_xb], dma_slot="xs0", tag="x3dma")
            for h in range(NH):
                wo_ap, B_woc = WOc(h)
                for b4 in range(4):
                    OP("tensor", lambda e, h=h, b4=b4, ms_ap=ms_ap,
                       wo_ap=wo_ap: e.matmul(
                        ps_wo[:, 512 * b4:512 * (b4 + 1)],
                        ms_ap[:, 128 * h:128 * (h + 1)],
                        wo_ap[:, 512 * b4:512 * (b4 + 1)],
                        start=(h == 0), stop=(h == NH - 1)),
                       reads=[B_ms, B_woc], writes=[B_pswo], tag="womm")
            x2b, B_x2b = xst[1]
            OP("vector", lambda e, x2b=x2b, xb=xb: e.tensor_add(x2b, ps_wo, xb),
               reads=[B_pswo, B_xb], writes=[B_x2b], tag="x2ev")
            OP("vector", lambda e, x2b=x2b: e.tensor_add(x2b, x2b, borep),
               reads=[B_borep], writes=[B_x2b], tag="x2ev")
            OP("sync", lambda e, t=t, x2b=x2b: e.dma_start(
                out=y[128 * t:128 * (t + 1), :], in_=x2b),
               reads=[B_x2b], writes=[("dram_y", 0, 1)], dma_slot="yw", tag="ywr")
            ln_tile(x2b, B_x2b, ln2st, B_ln2st, wlsc, B_wlsc, tag="ln2")
            for half in range(2):
                for k8 in range(8):
                    k = 8 * half + k8
                    OP("tensor", lambda e, k=k, k8=k8: e.transpose(
                        pt_bf[:, 128 * k8:128 * (k8 + 1)],
                        ln2st[:, 128 * k:128 * (k + 1)], ident_bf),
                       reads=[B_ln2st, B_identb], writes=[B_ptbf], tag="ln2tr")
                for k8 in range(8):
                    k = 8 * half + k8
                    dap, B_d2 = LN2c(k)
                    OP("vector", lambda e, dap=dap, k8=k8, t=t: e.tensor_copy(
                        dap[:, 128 * t:128 * (t + 1)], pt_bf[:, 128 * k8:128 * (k8 + 1)]),
                       reads=[B_ptbf],
                       writes=[(B_d2[0], B_d2[1] + 256 * t, B_d2[1] + 256 * (t + 1))],
                       tag="ln2te")

        ps_cs, B_pscs = PF(2048, 512)

        def colsum_rows(src_chunks, nk, tag):
            for bb in range(2):
                for k in range(nk):
                    cap, B_c = src_chunks(k)
                    OP("tensor", lambda e, cap=cap, bb=bb, k=k: e.matmul(
                        ps_cs[0:1, 0:512], ones128[:, 0:1],
                        cap[:, 512 * bb:512 * (bb + 1)],
                        start=(k == 0), stop=(k == nk - 1)),
                       reads=[B_c, B_ones128], writes=[B_pscs], tag=tag)
                OP("vector", lambda e, bb=bb: e.tensor_scalar_mul(
                    cs_f[:, 512 * bb:512 * (bb + 1)], ps_cs[0:1, 0:512], -128.0),
                   reads=[B_pscs], writes=[B_csf], tag=tag)
            OP("vector", lambda e: e.tensor_copy(cs_f2, cs_f), reads=[B_csf],
               writes=[B_csf2], tag=tag)
            hi_lo_rows(cs_f2, B_csf2, cs_hi, B_cshi, cs_lo, B_cslo, 1024)

        ps_g, B_psg = PF(0, 1024)
        ps_u, B_psu = PF(1024, 1024)
        ps_d = [PF(0, 1024), PF(1024, 1024)]
        ps_tr = [PF(2048, 512), PF(2048, 512)]

        if run_p4:
            colsum_rows(LN2c, 16, "cs1")
            for m in range(44):
                wg_ap, B_wg = wgbuf[m % 2]
                for which, dW4 in (("g", d["wgT"]), ("u", d["wuT"])):
                    OP("gpsimd", lambda e, m=m, dW4=dW4, wg_ap=wg_ap: e.dma_start(
                        out=wg_ap.rearrange("p (k c) -> p k c", k=16),
                        in_=dW4[:, 128 * m:128 * (m + 1)].rearrange(
                            "(k p) c -> p k c", p=128)),
                       writes=[B_wg], dma_slot=f"w{which}{m % 2}", tag="ffnwdma")
                    pp, B_pp = (ps_g, B_psg) if which == "g" else (ps_u, B_psu)
                    for bb in range(2):
                        OP("tensor", lambda e, bb=bb, pp=pp: e.matmul(
                            pp[:, 512 * bb:512 * (bb + 1)], ones_row,
                            cs_hi[:, 512 * bb:512 * (bb + 1)], start=True, stop=False),
                           reads=[B_ones, B_cshi], writes=[B_pp], tag="gseed")
                        OP("tensor", lambda e, bb=bb, pp=pp: e.matmul(
                            pp[:, 512 * bb:512 * (bb + 1)], ones_row,
                            cs_lo[:, 512 * bb:512 * (bb + 1)], start=False, stop=False),
                           reads=[B_ones, B_cslo], writes=[B_pp], tag="gseed")
                    for k in range(16):
                        lap, B_l2 = LN2c(k)
                        for bb in range(2):
                            OP("tensor", lambda e, k=k, bb=bb, wg_ap=wg_ap, lap=lap,
                               pp=pp: e.matmul(
                                pp[:, 512 * bb:512 * (bb + 1)],
                                wg_ap[:, 128 * k:128 * (k + 1)],
                                lap[:, 512 * bb:512 * (bb + 1)],
                                start=False, stop=(k == 15)),
                               reads=[B_wg, B_l2], writes=[B_pp], tag="gumm")
                    if which == "g":
                        OP("scalar", lambda e, m=m: e.activation(
                            silu_g, ps_g, AF.Silu, scale=sgup[:, m:m + 1]),
                           reads=[B_psg, B_sgup], writes=[B_silug], tag="silu")
                    else:
                        gu_ap, B_gu = GU(m)
                        OP("vector", lambda e, m=m, gu_ap=gu_ap: e.scalar_tensor_tensor(
                            out=gu_ap, in0=ps_u, scalar=sgup[:, 44 + m:45 + m],
                            in1=silu_g, op0=ALU.mult, op1=ALU.mult),
                           reads=[B_psu, B_sgup, B_silug], writes=[B_gu], tag="guev")

            colsum_rows(GU, 44, "cs2")
            for m in range(16):
                wd_ap, B_wd = wdbuf[m % 2]
                OP("gpsimd", lambda e, m=m, wd_ap=wd_ap: e.dma_start(
                    out=wd_ap.rearrange("p (k c) -> p k c", k=44),
                    in_=d["wdT"][:, 128 * m:128 * (m + 1)].rearrange(
                        "(k p) c -> p k c", p=128)),
                   writes=[B_wd], dma_slot=f"wd{m % 2}", tag="wddma")
                pp, B_pp = ps_d[m % 2]
                for bb in range(2):
                    OP("tensor", lambda e, bb=bb, pp=pp: e.matmul(
                        pp[:, 512 * bb:512 * (bb + 1)], ones_row,
                        cs_hi[:, 512 * bb:512 * (bb + 1)], start=True, stop=False),
                       reads=[B_ones, B_cshi], writes=[B_pp], tag="dseed")
                    OP("tensor", lambda e, bb=bb, pp=pp: e.matmul(
                        pp[:, 512 * bb:512 * (bb + 1)], ones_row,
                        cs_lo[:, 512 * bb:512 * (bb + 1)], start=False, stop=False),
                       reads=[B_ones, B_cslo], writes=[B_pp], tag="dseed")
                for k in range(44):
                    gu_ap, B_gu = GU(k)
                    for bb in range(2):
                        OP("tensor", lambda e, k=k, bb=bb, wd_ap=wd_ap, gu_ap=gu_ap,
                           pp=pp: e.matmul(
                            pp[:, 512 * bb:512 * (bb + 1)],
                            wd_ap[:, 128 * k:128 * (k + 1)],
                            gu_ap[:, 512 * bb:512 * (bb + 1)],
                            start=False, stop=(k == 43)),
                           reads=[B_wd, B_gu], writes=[B_pp], tag="dmm")
                OP("vector", lambda e, m=m, pp=pp: e.tensor_scalar(
                    dstg, pp, sdown[:, m:m + 1], None, ALU.mult),
                   reads=[B_pp, B_sdown], writes=[B_dstg], tag="dsc")
                for half in range(2):
                    pt_ap, B_pt4 = ps_tr[half]
                    for jj in range(4):
                        tt4 = 4 * half + jj
                        OP("tensor", lambda e, tt4=tt4, jj=jj, pt_ap=pt_ap: e.transpose(
                            pt_ap[:, 128 * jj:128 * (jj + 1)],
                            dstg[:, 128 * tt4:128 * (tt4 + 1)], ident_f),
                           reads=[B_dstg, B_identf], writes=[B_pt4], tag="dtr")
                    OP("vector", lambda e, half=half, pt_ap=pt_ap: e.tensor_copy(
                        ytile[:, 512 * half:512 * (half + 1)], pt_ap),
                       reads=[B_pt4], writes=[B_yt], tag="dtrev")
                OP("gpsimd", lambda e, m=m: e.dma_start(
                    out=y.rearrange("(t p) c -> p t c", p=128)[:, :, 128 * m:128 * (m + 1)],
                    in_=ytile.rearrange("p (t c) -> p t c", t=8),
                    accum_op=ALU.add),
                   reads=[B_yt, ("dram_y", 0, 1)], writes=[("dram_y", 0, 1)],
                   dma_slot="yacc", tag="yacc")

        if "ck" in dbg:
            o = dout("dbg_ck", [128, 2048], BF16)
            OP("sync", lambda e, o=o: e.dma_start(out=o, in_=ck_fm), reads=[B_ckfm],
               dma_slot="out")
        if "cv" in dbg:
            o = dout("dbg_cv", [128, 2048], BF16)
            OP("sync", lambda e, o=o: e.dma_start(out=o, in_=cv_tm), reads=[B_cvtm],
               dma_slot="out")
        if "tabs" in dbg:
            o = dout("dbg_tabB", [128, 1024])
            OP("sync", lambda e, o=o: e.dma_start(out=o, in_=tabB), reads=[B_tabB],
               dma_slot="out")
            o2 = dout("dbg_tabO", [128, 2 * NSLOT * 32])
            OP("sync", lambda e, o2=o2: e.dma_start(out=o2, in_=tabO), reads=[B_tabO],
               dma_slot="out")

        with nc.Block() as block:
            prog.finalize(block)
        for cm in prog._sem_ctxs:
            cm.__exit__(None, None, None)

    return nc, sorted(dbg_outs.keys())


def _host_tables():
    invf = np.exp(-np.arange(32, dtype=np.float64) * np.log(10000.0) / 32.0)
    return invf


def _prep_inputs(core, raw):
    x = np.ascontiguousarray(np.asarray(raw["L0_attn_hidden_states"],
                                        dtype=np.float32).reshape(S, HID))
    slots = slot_global_tiles(core)
    x_own = np.ascontiguousarray(
        np.concatenate([x[128 * g:128 * (g + 1), :] for g in slots], axis=0))
    pos_own = np.concatenate([np.arange(128 * g, 128 * (g + 1), dtype=np.float64)
                              for g in slots])
    invf = _host_tables()
    posT_own = np.ascontiguousarray(
        np.broadcast_to(pos_own[None, :], (32, NSLOT * 128))).astype(np.float32)
    kw_own = np.round(invf[:, None] * pos_own[None, :] / (2 * np.pi)
                      ).astype(np.float32)
    ends = (np.arange(1, C + 1, dtype=np.float64) * 4 - 1)
    posB = np.ascontiguousarray(
        np.broadcast_to(ends[None, :], (32, C))).astype(np.float32)
    kwB = np.round(invf[:, None] * ends[None, :] / (2 * np.pi)).astype(np.float32)

    own = own_global_tiles(core)
    cmax = np.empty((128, TPC), np.float32)
    p = np.arange(128)
    for j, g in enumerate(own):
        cmax[:, j] = np.floor((128 * g + p - 3) / 4.0).astype(np.float32)
    lm = np.zeros((TPC, 128, 256), np.float32)
    i = np.arange(128)[:, None]
    jj = np.arange(256)[None, :]
    band = (jj >= i) & (jj <= i + 128)
    for t, g in enumerate(own):
        m = band.copy()
        if g == 0:
            m = m & (jj >= 128)
        lm[t] = np.where(m, 0.0, NEG).astype(np.float32)
    lmask = np.ascontiguousarray(lm.transpose(1, 0, 2).reshape(128, TPC * 256))

    inp = {
        "x": x, "x_own": x_own, "posT_own": posT_own, "kw_own": kw_own,
        "posB": posB, "kwB": kwB, "cmax": cmax, "lmask": lmask,
        "Wq": np.asarray(raw["L0_attn_Wq"], np.float32),
        "bq": np.asarray(raw["L0_attn_bq"], np.float32),
        "Wcmp": np.asarray(raw["L0_attn_Wcmp"], np.float32),
        "bcmp": np.asarray(raw["L0_attn_bcmp"], np.float32),
        "Wk": np.asarray(raw["L0_attn_Wk"], np.float32),
        "bk": np.asarray(raw["L0_attn_bk"], np.float32),
        "Wv": np.asarray(raw["L0_attn_Wv"], np.float32),
        "bv": np.asarray(raw["L0_attn_bv"], np.float32),
        "Wlk": np.asarray(raw["L0_attn_Wlk"], np.float32),
        "blk": np.asarray(raw["L0_attn_blk"], np.float32),
        "Wlv": np.asarray(raw["L0_attn_Wlv"], np.float32),
        "blv": np.asarray(raw["L0_attn_blv"], np.float32),
        "qn_w": np.asarray(raw["L0_attn_qn_w"], np.float32),
        "kn_w": np.asarray(raw["L0_attn_kn_w"], np.float32),
        "Wo": np.asarray(raw["L0_attn_Wo"], np.float32),
        "bo": np.asarray(raw["L0_attn_bo"], np.float32),
        "wgT": np.ascontiguousarray(np.asarray(raw["L0_ffn_wq_gate"], np.int32).T),
        "s_gate": np.asarray(raw["L0_ffn_s_gate"], np.float32),
        "wuT": np.ascontiguousarray(np.asarray(raw["L0_ffn_wq_up"], np.int32).T),
        "s_up": np.asarray(raw["L0_ffn_s_up"], np.float32),
        "wdT": np.ascontiguousarray(np.asarray(raw["L0_ffn_wq_down"], np.int32).T),
        "s_down": np.asarray(raw["L0_ffn_s_down"], np.float32),
    }
    return inp


def kernel(**raw):
    nc, _ = build_program()
    in_maps = [_prep_inputs(c, raw) for c in range(NCORE)]
    res = run_bass_kernel_spmd(nc, in_maps, core_ids=list(range(NCORE)))
    y = np.zeros((S, HID), np.float32)
    for c in range(NCORE):
        shard = res.results[c]["y"]
        for j, g in enumerate(own_global_tiles(c)):
            y[128 * g:128 * (g + 1), :] = shard[128 * j:128 * (j + 1), :]
    return y.reshape(1, S, HID)


# version 10
# speedup vs baseline: 1.0074x; 1.0074x over previous
import os
from contextlib import ExitStack

import numpy as np
import ml_dtypes

import concourse.bass as bass
import concourse.mybir as mybir
from concourse.bass_utils import run_bass_kernel_spmd

F32 = mybir.dt.float32
BF16 = mybir.dt.bfloat16
I32 = mybir.dt.int32
AF = mybir.ActivationFunctionType
ALU = mybir.AluOpType
AX = mybir.AxisListType
BF = ml_dtypes.bfloat16

S = 8192
HID = 2048
NH = 16
D = 128
RATIO = 4
C = S // RATIO
WIN = 128
FFN = 5632
NEG = -3.0e38
EPS_LN = 1e-5
EPS_RMS = 1e-6
SCALE = float(np.float32(1.0 / np.float32(np.sqrt(np.float32(128.0)))))
PI = float(np.pi)
TWO_PI = float(2.0 * np.pi)
LN1E4_32 = float(np.log(10000.0) / 32.0)
NCORE = 8
NT = 64
TPC = 8
NPAIR = 4
NSLOT = 12
HG = 4
NHG = NH // HG
PAIR_W = [512, 1024, 1536, 2048]
TILE_W = [PAIR_W[k] for k in range(NPAIR) for _ in range(2)]
KB = 1024

MEGAB_BYTES = 136 * KB
MEGAF_BYTES = 44 * KB
SMALL_COLS = 2432


def core_pairs(c):
    return [c, 15 - c, 16 + c, 31 - c]


def own_global_tiles(c):
    out = []
    for p in core_pairs(c):
        out += [2 * p, 2 * p + 1]
    return out


def slot_global_tiles(c):
    out = []
    for p in core_pairs(c):
        h = 2 * p - 1
        if h < 0:
            h = 0
        out += [h, 2 * p, 2 * p + 1]
    return out


class Prog:
    def __init__(self, nc):
        self.nc = nc
        self.steps = []
        self.ivals = {}
        self._last_dma = {}

    def _deps_for(self, reads, writes):
        deps = set()
        for (sp, lo, hi) in list(reads) + list(writes):
            for iv in self.ivals.get(sp, []):
                if iv[0] < hi and lo < iv[1] and iv[2] is not None:
                    deps.add(iv[2])
        for (sp, lo, hi) in writes:
            for iv in self.ivals.get(sp, []):
                if iv[0] < hi and lo < iv[1]:
                    deps.update(iv[3])
        return deps

    def _update(self, sid, reads, writes):
        for (sp, lo, hi) in writes:
            lst = self.ivals.setdefault(sp, [])
            keep = []
            for iv in lst:
                if iv[0] < hi and lo < iv[1]:
                    if iv[0] < lo:
                        keep.append([iv[0], lo, iv[2], set(iv[3])])
                    if hi < iv[1]:
                        keep.append([hi, iv[1], iv[2], set(iv[3])])
                else:
                    keep.append(iv)
            keep.append([lo, hi, sid, set()])
            self.ivals[sp] = keep
        for (sp, lo, hi) in reads:
            for iv in self.ivals.get(sp, []):
                if iv[0] < hi and lo < iv[1]:
                    iv[3].add(sid)

    def op(self, eng, fn, reads=(), writes=(), dma_slot=None, tag=""):
        sid = len(self.steps)
        semkey = ("d", dma_slot) if dma_slot is not None else ("e", eng)
        deps = self._deps_for(reads, writes)
        if dma_slot is not None:
            prev = self._last_dma.get(dma_slot)
            if prev is not None:
                deps.add(prev)
            self._last_dma[dma_slot] = sid
        self.steps.append(dict(sid=sid, eng=eng, fn=fn, deps=deps, semkey=semkey,
                               dma=dma_slot is not None, needed=False, val=None,
                               tag=tag))
        self._update(sid, reads, writes)
        return sid

    def finalize(self, block):
        nc = self.nc
        steps = self.steps
        for s in steps:
            for dd in s["deps"]:
                steps[dd]["needed"] = True
        for s in steps:
            if s["dma"]:
                s["needed"] = True
        counts = {}
        for s in steps:
            if s["needed"]:
                inc = 16 if s["dma"] else 1
                v = counts.get(s["semkey"], 0) + inc
                counts[s["semkey"]] = v
                s["val"] = v
        semkeys = sorted(set(s["semkey"] for s in steps if s["needed"]))
        self._sem_ctxs = []
        self.sems = {}
        for i, sk in enumerate(semkeys):
            cm = nc.semaphore(f"s{i}")
            self.sems[sk] = cm.__enter__()
            self._sem_ctxs.append(cm)
        self.final_counts = counts
        per_eng = {e: [] for e in ("tensor", "vector", "scalar", "gpsimd", "sync")}
        for s in steps:
            per_eng[s["eng"]].append(s)
        eng_idx = {}
        cnt = {}
        for s in steps:
            if not s["dma"]:
                k = s["eng"]
                eng_idx[s["sid"]] = cnt.get(k, 0)
                cnt[k] = cnt.get(k, 0) + 1

        def mk_body(eng, mysteps):
            def body(e):
                waited = {}
                drain_cover = 0
                emitted = 0
                for s in mysteps:
                    want = {}
                    need_drain = False
                    for dd in s["deps"]:
                        ds = steps[dd]
                        if (not ds["dma"]) and ds["eng"] == eng:
                            if eng != "tensor" and eng_idx[dd] >= drain_cover:
                                need_drain = True
                            continue
                        sk, v = ds["semkey"], ds["val"]
                        if want.get(sk, 0) < v:
                            want[sk] = v
                    if need_drain:
                        e.drain()
                        drain_cover = emitted
                    for sk in sorted(want):
                        v = want[sk]
                        if waited.get(sk, 0) < v:
                            e.wait_ge(self.sems[sk], v)
                            waited[sk] = v
                    inst = s["fn"](e)
                    if s["needed"]:
                        inst.then_inc(self.sems[s["semkey"]], 16 if s["dma"] else 1)
                    if not s["dma"]:
                        emitted += 1
                if eng == "sync":
                    for sk in sorted(self.final_counts):
                        if sk[0] == "d":
                            v = self.final_counts[sk]
                            if waited.get(sk, 0) < v:
                                e.wait_ge(self.sems[sk], v)
            return body

        block.sync(mk_body("sync", per_eng["sync"]))
        block.tensor(mk_body("tensor", per_eng["tensor"]))
        block.vector(mk_body("vector", per_eng["vector"]))
        block.scalar(mk_body("scalar", per_eng["scalar"]))
        block.gpsimd(mk_body("gpsimd", per_eng["gpsimd"]))


PHASES = ("c", "p0", "p2", "p3", "p4a", "p4b")
PH = {p: 1 << i for i, p in enumerate(PHASES)}
ALL = (1 << len(PHASES)) - 1


def phmask(phases):
    if phases == "all":
        return ALL
    m = 0
    for p in phases.split("+"):
        m |= PH[p]
    return m


class Arena:
    def __init__(self, name, nbytes, align=64):
        self.name = name
        self.nbytes = nbytes
        self.align = align
        self.regions = []

    def alloc(self, name, nbytes, phases):
        mask = phmask(phases)
        nbytes = (nbytes + self.align - 1) // self.align * self.align
        taken = sorted((lo, hi) for lo, hi, m, _ in self.regions if m & mask)
        pos = 0
        for lo, hi in taken:
            if pos + nbytes <= lo:
                break
            pos = max(pos, hi)
        assert pos + nbytes <= self.nbytes, \
            f"{self.name}: out of space for {name} ({nbytes}B, phases={phases})"
        self.regions.append((pos, pos + nbytes, mask, name))
        return pos


def build_program(debug=(), stop_after=None):
    nc = bass.Bass("TRN2", target_bir_lowering=False)
    dbg = set(debug)
    run_p4 = stop_after != "p3"

    def din(name, shape, dt):
        return nc.dram_tensor(name, shape, dt, kind="ExternalInput").ap()

    d = {}
    d["x"] = din("x", [S, HID], F32)
    d["x_own"] = din("x_own", [NSLOT * 128, HID], F32)
    d["posT_own"] = din("posT_own", [32, NSLOT * 128], F32)
    d["kw_own"] = din("kw_own", [32, NSLOT * 128], F32)
    d["posB"] = din("posB", [32, C], F32)
    d["kwB"] = din("kwB", [32, C], F32)
    d["cmax"] = din("cmax", [128, TPC], F32)
    d["lmask"] = din("lmask", [128, TPC * 256], F32)
    for nm, sh in [("Wq", [HID, NH * D]), ("Wcmp", [HID, 1]), ("Wk", [HID, D]),
                   ("Wv", [HID, D]), ("Wlk", [HID, NH * D]), ("Wlv", [HID, NH * D]),
                   ("Wo", [NH * D, HID])]:
        d[nm] = din(nm, sh, F32)
    for nm, sh in [("bq", NH * D), ("bcmp", 1), ("bk", D), ("bv", D),
                   ("blk", NH * D), ("blv", NH * D), ("qn_w", D), ("kn_w", D),
                   ("bo", HID), ("s_gate", FFN), ("s_up", FFN), ("s_down", HID)]:
        d[nm] = din(nm, [sh], F32)
    d["wgT"] = din("wgT", [HID, FFN], I32)
    d["wuT"] = din("wuT", [HID, FFN], I32)
    d["wdT"] = din("wdT", [FFN, HID], I32)

    y = nc.dram_tensor("y", [TPC * 128, HID], F32, kind="ExternalOutput").ap()
    merged_hbm = nc.dram_tensor(
        "merged_scratch", [NH, 128, 1024], BF16,
        kind="ExternalOutput" if "merged" in dbg else "Internal").ap()
    lnf_hbm = nc.dram_tensor("lnf_scratch", [NSLOT, 128, 2048], BF16,
                             kind="Internal").ap()

    dbg_outs = {}

    def dout(name, shape, dt=F32):
        dbg_outs[name] = nc.dram_tensor(name, shape, dt, kind="ExternalOutput").ap()
        return dbg_outs[name]

    with ExitStack() as ctx:
        megaB_h = ctx.enter_context(
            nc.sbuf_tensor("megaB", [128, MEGAB_BYTES // 2], BF16))
        megaF_h = ctx.enter_context(
            nc.sbuf_tensor("megaF", [128, MEGAF_BYTES // 4], F32))
        small_h = ctx.enter_context(nc.sbuf_tensor("small", [128, SMALL_COLS], F32))
        psF_h = ctx.enter_context(nc.psum_tensor("psF", [128, 2560], F32))
        psB_h = ctx.enter_context(nc.psum_tensor("psB", [128, 3072], BF16))

        prog = Prog(nc)
        OP = prog.op
        AB = Arena("megaB", MEGAB_BYTES, align=64)
        AFF = Arena("megaF", MEGAF_BYTES, align=64)

        def MBr(name, cols, phases, part=128):
            lo = AB.alloc(name, cols * 2, phases)
            return megaB_h[0:part, lo // 2: lo // 2 + cols], ("megaB", lo, lo + 2 * cols)

        def MFr(name, cols, phases, part=128):
            lo = AFF.alloc(name, cols * 4, phases)
            return megaF_h[0:part, lo // 4: lo // 4 + cols], ("megaF", lo, lo + 4 * cols)

        def MBrow(name, cols, phases):
            return MBr(name, cols, phases, part=1)

        def MFrow(name, cols, phases):
            return MFr(name, cols, phases, part=1)

        small_off = [0]

        def SC(cols):
            lo = small_off[0]
            small_off[0] += cols
            assert small_off[0] <= SMALL_COLS
            return small_h[:, lo:lo + cols], ("small", lo * 4, (lo + cols) * 4)

        def PF(col_lo, cols):
            return psF_h[:, col_lo:col_lo + cols], \
                ("psF", col_lo * 4, (col_lo + cols) * 4)

        def PB(col_lo, cols):
            return psB_h[:, col_lo:col_lo + cols], \
                ("psB", col_lo * 2, (col_lo + cols) * 2)

        ident_bf, B_identb = MBr("ident_bf", 128, "all")
        ones128, B_ones128 = MBr("ones128", 128, "all")
        ones_row, B_ones = MBrow("ones_row", 128, "all")
        pats, B_pats = MBr("pats", 4 * 128, "c+p0")
        wcmpr, B_wcmpr = MBr("wcmpr", 2048, "c+p0")
        lmask, B_lmask = MBr("lmask", TPC * 256, "c+p0+p2")
        ck_fm, B_ckfm = MBr("ck_fm", 2048, "c+p0+p2")
        cv_tm, B_cvtm = MBr("cv_tm", 2048, "c+p0+p2")
        GU_lo = AB.alloc("GU", 44 * 1024 * 2, "p4a+p4b")

        def GU(mm_):
            lo = GU_lo + 2 * KB * mm_
            return megaB_h[:, lo // 2: lo // 2 + 1024], ("megaB", lo, lo + 2 * KB)

        wlsc, B_wlsc = MBr("wlsc", 2048, "p0+p3")

        ident_f, B_identf = MFr("ident_f", 128, "all")
        knrep, B_knrep = MFr("knrep", 128, "c+p0+p2")
        qnrep, B_qnrep = MFr("qnrep", 128, "c+p0+p2")
        tabO, B_tabO = MFr("tabO", 2 * NSLOT * 32, "c+p0+p2")
        tabB, B_tabB = MFr("tabB", 2 * 16 * 32, "c+p0")

        ENT, B_ENTx = MBr("ENT", 4 * 2048, "p0")
        efm, B_efm = MBr("efm", 16 * 128, "p0")
        wkv, B_wkv = MBr("wkv", 16 * 256, "c+p0")
        lnst, B_lnst = MBr("lnst", 2048, "p0")
        lnsto, B_lnsto = MBr("lnsto", 2048, "p0")
        lnf_wr, B_lnfw = MBr("lnf_wr", 2048, "p0")
        lnf_stg = [MBr(f"lnf_stg{i}", 2048, "p2") for i in range(2)]

        xst = [MFr(f"xst{i}", 2048, "p3") for i in range(2)]
        xbf = [MBr(f"xbf{i}", 2048, "p0") for i in range(2)]
        angT, B_angT = MFr("angT", 2048, "c", part=32)
        posBs, B_posBs = MFr("posBs", 2048, "c", part=32)
        r_wcmp32, B_rw32 = MFrow("r_wcmp32", 2048, "c")
        r_wcmpb, B_rwb = MBrow("r_wcmpb", 2048, "c")
        r_bkv, B_rbkv = MFrow("r_bkv", 256, "c+p0")
        r_bkv_hi, B_rbkvh = MBrow("r_bkv_hi", 256, "c+p0")
        r_bkv_lo, B_rbkvl = MBrow("r_bkv_lo", 256, "c+p0")
        r_kn, B_rkn = MFrow("r_kn", 128, "c")
        r_qn, B_rqn = MFrow("r_qn", 128, "c")
        r_bcmp, B_rbc = MFrow("r_bcmp", 1, "c+p0")
        r_bcmpb, B_rbcb = MBrow("r_bcmpb", 2, "c+p0")

        WQT_los = [AB.alloc(f"WQT{j}", 16 * 512 * 2, "p2") for j in range(2)]

        def WQT(j, k):
            lo = WQT_los[j] + 1024 * k
            return megaB_h[:, lo // 2: lo // 2 + 512], ("megaB", lo, lo + 1024)

        wq_alls = [(megaB_h[:, WQT_los[j] // 2: WQT_los[j] // 2 + 16 * 512],
                    ("megaB", WQT_los[j], WQT_los[j] + 16 * 1024))
                   for j in range(2)]
        qtm, B_qtm = MFr("qtm", 1024, "p2")
        qbf, B_qbf = MBr("qbf", 1024, "p2")
        brow_f, B_browf = MFrow("brow_f", 512, "p2")
        brow_hi, B_browhi = MBrow("brow_hi", 512, "p2")
        brow_lo, B_browlo = MBrow("brow_lo", 512, "p2")
        QG_lo = AB.alloc("QG", TPC * HG * 128 * 2, "p2")

        def QG(hh, t):
            lo = QG_lo + (hh * 8 + t) * 256
            return megaB_h[:, lo // 2: lo // 2 + 128], ("megaB", lo, lo + 256)

        LKG_lo = AB.alloc("LKG", NSLOT * HG * 128 * 2, "p2")

        def LKG(hh, sl):
            lo = LKG_lo + (hh * NSLOT + sl) * 256
            return megaB_h[:, lo // 2: lo // 2 + 128], ("megaB", lo, lo + 256)

        def LKGW(hh, sl):
            lo = LKG_lo + (hh * NSLOT + sl) * 256
            return megaB_h[:, lo // 2: lo // 2 + 256], ("megaB", lo, lo + 512)

        LVG_lo = AB.alloc("LVG", NSLOT * HG * 128 * 2, "p2")

        def LVG(sl, hh):
            lo = LVG_lo + (sl * HG + hh) * 256
            return megaB_h[:, lo // 2: lo // 2 + 128], ("megaB", lo, lo + 256)

        MSK_lo = AB.alloc("MSK", sum(TILE_W) * 2, "p2")
        msk_off = []
        _acc = 0
        for t in range(TPC):
            msk_off.append(_acc)
            _acc += TILE_W[t] * 2

        def MSK(t):
            lo = MSK_lo + msk_off[t]
            return megaB_h[:, lo // 2: lo // 2 + TILE_W[t]], \
                ("megaB", lo, lo + 2 * TILE_W[t])

        Pbuf, B_Pbuf = MBr("Pbuf", 2048, "p2")
        PT_lo = AB.alloc("PT", 2 * 2048 * 2, "p2")

        def PTv(par, cols):
            lo = PT_lo + par * 4 * KB
            return megaB_h[:, lo // 2: lo // 2 + cols], ("megaB", lo, lo + 2 * cols)

        plsb, B_plsb = MBr("plsb", 256, "p2")
        pltsb, B_pltsb = MBr("pltsb", 512, "p2")
        mrgb, B_mrgb = MBr("mrgb", 1024, "p2")
        S_sb, B_Ssb = MFr("S_sb", 2048, "p2")
        iota_msk, B_iotam = MFr("iota_msk", 2048, "p2")

        WO_lo = AB.alloc("WO", 16 * 2048 * 2, "p3")

        def WOc(k):
            lo = WO_lo + 4 * KB * k
            return megaB_h[:, lo // 2: lo // 2 + 2048], ("megaB", lo, lo + 4 * KB)

        wo_all = megaB_h[:, WO_lo // 2: WO_lo // 2 + 16 * 2048]
        B_WO_ALL = ("megaB", WO_lo, WO_lo + 64 * KB)
        mstg = [MBr(f"mstg{i}", 2048, "p3") for i in range(2)]
        ln2st, B_ln2st = MBr("ln2st", 2048, "p3")
        bo_f, B_bof = MFrow("bo_f", 2048, "p3")
        bo_hi, B_bohi = MBrow("bo_hi", 2048, "p3")
        bo_lo, B_bolo = MBrow("bo_lo", 2048, "p3")
        borep, B_borep = MFr("borep", 2048, "p3")
        LN2_lo = AB.alloc("LN2", 16 * 1024 * 2, "p3+p4a")

        def LN2c(k):
            lo = LN2_lo + 2 * KB * k
            return megaB_h[:, lo // 2: lo // 2 + 1024], ("megaB", lo, lo + 2 * KB)

        wgbuf = [MBr(f"wg{i}", 2048, "p4a") for i in range(2)]
        wdbuf = [MBr(f"wd{i}", 5632, "p4b") for i in range(2)]
        silu_g, B_silug = MBr("silu_g", 1024, "p4a")
        cs_f, B_csf = MFrow("cs_f", 1024, "p4a+p4b")
        cs_f2, B_csf2 = MFrow("cs_f2", 1024, "p4a+p4b")
        cs_hi, B_cshi = MBrow("cs_hi", 1024, "p4a+p4b")
        cs_lo, B_cslo = MBrow("cs_lo", 1024, "p4a+p4b")
        dstg, B_dstg = MFr("dstg", 1024, "p4b")
        ytile, B_yt = MFr("ytile", 1024, "p4b")

        bn2, B_bn2 = SC(2)
        lncols, B_lncols = SC(4)
        wcc, B_wcc = SC(8)
        cmax_sb, B_cmax = SC(TPC)
        bcmpc, B_bcmpc = SC(1)
        idsc, B_idsc = SC(128)
        ckt, B_ckt = SC(256)
        rmsc, B_rmsc = SC(2)
        ropet, B_ropet = SC(64)
        wrap0, B_wrap0 = SC(512)
        wrap1, B_wrap1 = SC(512)
        tv8, B_tv8 = SC(8)
        st8, B_st8 = SC(8)
        rms4, B_rms4 = SC(4)
        slsb, B_slsb = SC(256)
        sgup, B_sgup = SC(88)
        sdown, B_sdown = SC(16)
        pio, B_pio = SC(128)
        pt1, B_pt1 = SC(128)
        pt2c, B_pt2c = SC(128)
        wdiag = [MBr("wdiagA", 128, "p0"), MBr("wdiagB", 128, "p0")]
        iv32, B_iv32 = SC(1)
        iv32 = small_h[0:32, B_iv32[1] // 4: B_iv32[1] // 4 + 1]

        def IOTA(out_ap, wb, pattern, base, cm):
            OP("gpsimd", lambda e: e.iota(out_ap, pattern=pattern, base=base,
                                          channel_multiplier=cm,
                                          allow_small_or_imprecise_dtypes=True),
               writes=[wb], tag="iota")

        IOTA(idsc, B_idsc, [[-1, 128]], 0, 1)
        OP("vector", lambda e: e.tensor_scalar(ident_bf, idsc, 0.0, None, ALU.is_equal),
           reads=[B_idsc], writes=[B_identb])
        OP("vector", lambda e: e.tensor_scalar(ident_f, idsc, 0.0, None, ALU.is_equal),
           reads=[B_idsc], writes=[B_identf])
        OP("vector", lambda e: e.memset(ones_row, 1.0), writes=[B_ones])
        OP("vector", lambda e: e.memset(ones128, 1.0), writes=[B_ones128])
        IOTA(pio, B_pio, [[-4, 128]], 0, 1)
        for j in range(4):
            pat = pats[:, 128 * j:128 * (j + 1)]
            OP("vector", lambda e, j=j: e.tensor_scalar(pt1, pio, float(-128 * j),
                                                        None, ALU.is_ge),
               reads=[B_pio], writes=[B_pt1], tag="pat")
            OP("vector", lambda e, j=j: e.tensor_scalar(pt2c, pio, float(-128 * j + 3),
                                                        None, ALU.is_le),
               reads=[B_pio], writes=[B_pt2c], tag="pat")
            OP("vector", lambda e, pat=pat: e.tensor_mul(pat, pt1, pt2c),
               reads=[B_pt1, B_pt2c], writes=[B_pats], tag="pat")

        OP("sync", lambda e: e.dma_start(out=r_wcmp32,
                                         in_=d["Wcmp"].rearrange("h one -> one h")),
           writes=[B_rw32], dma_slot="ld0")
        OP("vector", lambda e: e.tensor_copy(r_wcmpb, r_wcmp32),
           reads=[B_rw32], writes=[B_rwb])
        pW, B_pW = PF(0, 2048)
        for q in range(4):
            OP("tensor", lambda e, q=q: e.matmul(pW[:, 512 * q:512 * (q + 1)], ones_row,
                                                 r_wcmpb[:, 512 * q:512 * (q + 1)],
                                                 start=True, stop=True),
               reads=[B_ones, B_rwb], writes=[B_pW])
        OP("vector", lambda e: e.tensor_copy(wcmpr, pW), reads=[B_pW], writes=[B_wcmpr])
        OP("sync", lambda e: e.dma_start(out=r_bcmp, in_=d["bcmp"][None, :]),
           writes=[B_rbc], dma_slot="ld0")
        OP("vector", lambda e: e.tensor_copy(r_bcmpb[:, 0:1], r_bcmp),
           reads=[B_rbc], writes=[B_rbcb])
        pB1, B_pB1 = PF(2048, 64)
        OP("tensor", lambda e: e.matmul(pB1[:, 0:1], ones_row, r_bcmpb[:, 0:1],
                                        start=True, stop=True),
           reads=[B_ones, B_rbcb], writes=[B_pB1])
        OP("vector", lambda e: e.tensor_copy(bcmpc, pB1[:, 0:1]),
           reads=[B_pB1], writes=[B_bcmpc])

        def hi_lo_rows(f_ap, B_f, hi_ap, B_hi, lo_ap, B_lo, n):
            OP("vector", lambda e: e.tensor_copy(hi_ap, f_ap), reads=[B_f], writes=[B_hi])
            OP("vector", lambda e: e.tensor_sub(f_ap, f_ap, hi_ap),
               reads=[B_hi], writes=[B_f])
            OP("vector", lambda e: e.tensor_copy(lo_ap, f_ap), reads=[B_f], writes=[B_lo])

        def replicate_f32(dst_ap, B_dst, hi_ap, B_hi, lo_ap, B_lo, cols, pcol,
                          scale=None):
            pp, B_pp = PF(pcol, cols)
            for q in range((cols + 511) // 512):
                sl = slice(512 * q, min(cols, 512 * (q + 1)))
                OP("tensor", lambda e, sl=sl: e.matmul(pp[:, sl], ones_row, hi_ap[:, sl],
                                                       start=True, stop=False),
                   reads=[B_ones, B_hi], writes=[B_pp])
                OP("tensor", lambda e, sl=sl: e.matmul(pp[:, sl], ones_row, lo_ap[:, sl],
                                                       start=False, stop=True),
                   reads=[B_ones, B_lo], writes=[B_pp])
            if scale is None:
                OP("vector", lambda e: e.tensor_copy(dst_ap, pp), reads=[B_pp],
                   writes=[B_dst])
            else:
                OP("vector", lambda e: e.tensor_scalar_mul(dst_ap, pp, scale),
                   reads=[B_pp], writes=[B_dst])

        OP("sync", lambda e: e.dma_start(out=r_kn, in_=d["kn_w"][None, :]),
           writes=[B_rkn], dma_slot="ld0")
        OP("sync", lambda e: e.dma_start(out=r_qn, in_=d["qn_w"][None, :]),
           writes=[B_rqn], dma_slot="ld0")
        knh, B_knh = MBrow("knh", 128, "c")
        knl, B_knl = MBrow("knl", 128, "c")
        qnh, B_qnh = MBrow("qnh", 128, "c")
        qnl, B_qnl = MBrow("qnl", 128, "c")
        hi_lo_rows(r_kn, B_rkn, knh, B_knh, knl, B_knl, 128)
        hi_lo_rows(r_qn, B_rqn, qnh, B_qnh, qnl, B_qnl, 128)
        replicate_f32(knrep, B_knrep, knh, B_knh, knl, B_knl, 128, 2112)
        replicate_f32(qnrep, B_qnrep, qnh, B_qnh, qnl, B_qnl, 128, 2112, scale=SCALE)

        OP("sync", lambda e: e.dma_start(out=r_bkv[:, 0:128], in_=d["bk"][None, :]),
           writes=[B_rbkv], dma_slot="ld0")
        OP("sync", lambda e: e.dma_start(out=r_bkv[:, 128:256], in_=d["bv"][None, :]),
           writes=[B_rbkv], dma_slot="ld0")
        hi_lo_rows(r_bkv, B_rbkv, r_bkv_hi, B_rbkvh, r_bkv_lo, B_rbkvl, 256)

        OP("sync", lambda e: e.dma_start(out=cmax_sb, in_=d["cmax"][:, :]),
           writes=[B_cmax], dma_slot="ld0")
        OP("gpsimd", lambda e: e.dma_start(out=lmask, in_=d["lmask"][:, :]),
           writes=[B_lmask], dma_slot="ldg")
        OP("sync", lambda e: e.dma_start(out=sgup[:, 0:44],
                                         in_=d["s_gate"].rearrange("(m p) -> p m", p=128),
                                         allow_slow_non_contiguous=True),
           writes=[B_sgup], dma_slot="ld0")
        OP("sync", lambda e: e.dma_start(out=sgup[:, 44:88],
                                         in_=d["s_up"].rearrange("(m p) -> p m", p=128),
                                         allow_slow_non_contiguous=True),
           writes=[B_sgup], dma_slot="ld0")
        OP("sync", lambda e: e.dma_start(out=sdown,
                                         in_=d["s_down"].rearrange("(m p) -> p m", p=128),
                                         allow_slow_non_contiguous=True),
           writes=[B_sdown], dma_slot="ld0")
        OP("gpsimd", lambda e: e.dma_start(
            out=wkv.rearrange("p (k c) -> p k c", k=16)[:, :, 0:128],
            in_=d["Wk"].rearrange("(k p) c -> p k c", p=128)),
           writes=[B_wkv], dma_slot="ldg")
        OP("gpsimd", lambda e: e.dma_start(
            out=wkv.rearrange("p (k c) -> p k c", k=16)[:, :, 128:256],
            in_=d["Wv"].rearrange("(k p) c -> p k c", p=128)),
           writes=[B_wkv], dma_slot="ldg")

        IOTA(iv32, B_iv32, [[0, 1]], 0, 1)
        OP("scalar", lambda e: e.activation(iv32, iv32, AF.Exp, scale=-LN1E4_32),
           reads=[B_iv32], writes=[B_iv32])
        ptr, B_ptr = PF(2048, 512)

        def build_tables(pos_ap, B_pos, kw_ap, B_kw, n, dst_ap, B_dst):
            OP("vector", lambda e: e.tensor_scalar(angT[:, 0:n * 128], pos_ap, iv32,
                                                   None, ALU.mult),
               reads=[B_pos, B_iv32], writes=[B_angT])
            OP("vector", lambda e: e.scalar_tensor_tensor(
                out=pos_ap, in0=kw_ap, scalar=-TWO_PI, in1=angT[:, 0:n * 128],
                op0=ALU.mult, op1=ALU.add),
               reads=[B_kw, B_angT], writes=[B_pos])
            done = 0
            while done < n:
                grp = min(16, n - done)
                g = 32 * grp
                for j in range(grp):
                    OP("tensor", lambda e, j=j, dn=done: e.transpose(
                        ptr[:, 32 * j:32 * (j + 1)],
                        pos_ap[:, 128 * (dn + j):128 * (dn + j + 1)],
                        ident_f[0:32, 0:32]),
                       reads=[B_pos, B_identf], writes=[B_ptr], tag="ttab")
                co = dst_ap[:, 32 * done:32 * done + g]
                so = dst_ap[:, n * 32 + 32 * done:n * 32 + 32 * done + g]
                for shift, outa in ((0.0, so), (PI / 2, co)):
                    if shift == 0.0:
                        OP("vector", lambda e, g=g: e.tensor_copy(wrap0[:, 0:g],
                                                                  ptr[:, 0:g]),
                           reads=[B_ptr], writes=[B_wrap0])
                    else:
                        OP("vector", lambda e, g=g, sh=shift: e.tensor_scalar_add(
                            wrap0[:, 0:g], ptr[:, 0:g], sh),
                           reads=[B_ptr], writes=[B_wrap0])
                    for cmp_op, thr, delta in ((ALU.is_gt, PI, -TWO_PI),
                                               (ALU.is_lt, -PI, TWO_PI)):
                        OP("vector", lambda e, g=g, co_=cmp_op, th=thr:
                           e.tensor_scalar(wrap1[:, 0:g], wrap0[:, 0:g], th, None,
                                           co_),
                           reads=[B_wrap0], writes=[B_wrap1])
                        OP("vector", lambda e, g=g, dl=delta: e.tensor_scalar_mul(
                            wrap1[:, 0:g], wrap1[:, 0:g], dl),
                           reads=[B_wrap1], writes=[B_wrap1])
                        OP("vector", lambda e, g=g: e.tensor_add(
                            wrap0[:, 0:g], wrap0[:, 0:g], wrap1[:, 0:g]),
                           reads=[B_wrap0, B_wrap1], writes=[B_wrap0])
                    OP("vector", lambda e, g=g: e.tensor_scalar_min(wrap0[:, 0:g],
                                                                    wrap0[:, 0:g], PI),
                       reads=[B_wrap0], writes=[B_wrap0])
                    OP("vector", lambda e, g=g: e.tensor_scalar_max(wrap0[:, 0:g],
                                                                    wrap0[:, 0:g], -PI),
                       reads=[B_wrap0], writes=[B_wrap0])
                    OP("scalar", lambda e, g=g, oa=outa: e.activation(oa, wrap0[:, 0:g],
                                                                      AF.Sin),
                       reads=[B_wrap0], writes=[B_dst])
                done += grp

        OP("sync", lambda e: e.dma_start(out=posBs, in_=d["posB"][:, :]),
           writes=[B_posBs], dma_slot="ld0")
        kw_in, B_kwin = MFr("kw_in", 2048, "c", part=32)
        OP("sync", lambda e: e.dma_start(out=kw_in, in_=d["kwB"][:, :]),
           writes=[B_kwin], dma_slot="ld0")
        build_tables(posBs, B_posBs, kw_in, B_kwin, 16, tabB, B_tabB)
        OP("sync", lambda e: e.dma_start(out=posBs[:, 0:NSLOT * 128],
                                         in_=d["posT_own"][:, :]),
           writes=[B_posBs], dma_slot="ld0")
        OP("sync", lambda e: e.dma_start(out=kw_in[:, 0:NSLOT * 128],
                                         in_=d["kw_own"][:, :]),
           writes=[B_kwin], dma_slot="ld0")
        build_tables(posBs[:, 0:NSLOT * 128], B_posBs, kw_in[:, 0:NSLOT * 128],
                     B_kwin, NSLOT, tabO, B_tabO)
        if "tabs" in dbg:
            o_tb = dout("dbg_tabB", [128, 1024])
            OP("sync", lambda e: e.dma_start(out=o_tb, in_=tabB), reads=[B_tabB],
               dma_slot="out")
            o_to = dout("dbg_tabO", [128, 2 * NSLOT * 32])
            OP("sync", lambda e: e.dma_start(out=o_to, in_=tabO), reads=[B_tabO],
               dma_slot="out")

        shuf1 = [i ^ 1 for i in range(32)]
        shuf2 = [i ^ 2 for i in range(32)]
        ent_ps, B_entps = PF(0, 2048)
        pt_bf, B_ptbf = PB(0, 1024)
        ck_ps, B_ckps = PF(2048, 256)

        def ln_tile(x_ap, B_x, out_ap, B_out, dump, B_dump, tag=""):
            sm = bn2[:, 0:1]
            ss = bn2[:, 1:2]
            inv = lncols[:, 0:1]
            nmi = lncols[:, 1:2]
            tt = lncols[:, 2:3]
            OP("vector", lambda e: e.reduce_sum(sm, x_ap, axis=AX.X),
               reads=[B_x], writes=[B_bn2], tag=tag)
            OP("vector", lambda e: e.memset(ss, 0.0), writes=[B_bn2], tag=tag)
            OP("vector", lambda e: e.scalar_tensor_tensor(
                out=dump, in0=x_ap, scalar=1.0, in1=x_ap,
                op0=ALU.bypass, op1=ALU.mult, accum_out=ss),
               reads=[B_x], writes=[B_dump, B_bn2], tag=tag)
            OP("vector", lambda e: e.tensor_scalar_mul(sm, sm, 1.0 / HID),
               reads=[B_bn2], writes=[B_bn2], tag=tag)
            OP("vector", lambda e: e.tensor_mul(tt, sm, sm),
               reads=[B_bn2], writes=[B_lncols], tag=tag)
            OP("vector", lambda e: e.tensor_scalar_mul(inv, ss, 1.0 / HID),
               reads=[B_bn2], writes=[B_lncols], tag=tag)
            OP("vector", lambda e: e.tensor_sub(inv, inv, tt),
               reads=[B_lncols], writes=[B_lncols], tag=tag)
            OP("vector", lambda e: e.tensor_scalar_add(inv, inv, EPS_LN),
               reads=[B_lncols], writes=[B_lncols], tag=tag)
            OP("scalar", lambda e: e.activation(inv, inv, AF.Sqrt),
               reads=[B_lncols], writes=[B_lncols], tag=tag)
            OP("vector", lambda e: e.reciprocal(inv, inv),
               reads=[B_lncols], writes=[B_lncols], tag=tag)
            OP("vector", lambda e: e.tensor_mul(nmi, sm, inv),
               reads=[B_bn2, B_lncols], writes=[B_lncols], tag=tag)
            OP("vector", lambda e: e.tensor_scalar_mul(nmi, nmi, -1.0),
               reads=[B_lncols], writes=[B_lncols], tag=tag)
            OP("scalar", lambda e: e.activation(out_ap, x_ap, AF.Identity, bias=nmi,
                                                scale=inv),
               reads=[B_x, B_lncols], writes=[B_out], tag=tag)

        def ENTt(i4):
            lo = B_ENTx[1] + 4096 * i4
            return megaB_h[:, lo // 2: lo // 2 + 2048], ("megaB", lo, lo + 4096)

        for gt in range(NT):
            xb, B_xb = xbf[gt % 2]
            OP("gpsimd", lambda e, gt=gt, xb=xb: e.dma_start(
                out=xb, in_=d["x"][128 * gt:128 * (gt + 1), :]),
               writes=[B_xb], dma_slot=f"xs{gt % 2}")
            ln_tile(xb, B_xb, lnst, B_lnst, wlsc, B_wlsc, tag="p0ln")
            wcol = wcc[:, 0:1]
            OP("vector", lambda e: e.memset(wcol, 0.0), writes=[B_wcc], tag="p0wz")
            OP("vector", lambda e: e.scalar_tensor_tensor(
                out=wlsc, in0=lnst, scalar=1.0, in1=wcmpr,
                op0=ALU.bypass, op1=ALU.mult, accum_out=wcol),
               reads=[B_lnst, B_wcmpr], writes=[B_wlsc, B_wcc], tag="p0wlog")
            OP("vector", lambda e: e.tensor_add(wcol, wcol, bcmpc),
               reads=[B_bcmpc], writes=[B_wcc])
            m4 = wcc[:, 1:2]; ee = wcc[:, 2:3]; s4 = wcc[:, 3:4]
            rr = wcc[:, 4:5]; t0 = wcc[:, 5:6]; nm4 = wcc[:, 6:7]
            OP("vector", lambda e: e.stream_shuffle(t0, wcol, shuf1), reads=[B_wcc], writes=[B_wcc])
            OP("vector", lambda e: e.tensor_max(m4, wcol, t0), reads=[B_wcc], writes=[B_wcc])
            OP("vector", lambda e: e.stream_shuffle(t0, m4, shuf2), reads=[B_wcc], writes=[B_wcc])
            OP("vector", lambda e: e.tensor_max(m4, m4, t0), reads=[B_wcc], writes=[B_wcc])
            OP("vector", lambda e: e.tensor_scalar_mul(nm4, m4, -1.0), reads=[B_wcc], writes=[B_wcc])
            OP("scalar", lambda e: e.activation(ee, wcol, AF.Exp, bias=nm4),
               reads=[B_wcc], writes=[B_wcc])
            OP("vector", lambda e: e.stream_shuffle(t0, ee, shuf1), reads=[B_wcc], writes=[B_wcc])
            OP("vector", lambda e: e.tensor_add(s4, ee, t0), reads=[B_wcc], writes=[B_wcc])
            OP("vector", lambda e: e.stream_shuffle(t0, s4, shuf2), reads=[B_wcc], writes=[B_wcc])
            OP("vector", lambda e: e.tensor_add(s4, s4, t0), reads=[B_wcc], writes=[B_wcc])
            OP("vector", lambda e: e.reciprocal(rr, s4), reads=[B_wcc], writes=[B_wcc])
            OP("vector", lambda e: e.tensor_mul(ee, ee, rr), reads=[B_wcc], writes=[B_wcc])
            j = gt % 4
            wd_ap, B_wd = wdiag[gt % 2]
            OP("vector", lambda e, j=j, wd_ap=wd_ap: e.tensor_scalar(
                wd_ap, pats[:, 128 * j:128 * (j + 1)], ee, None, ALU.mult),
               reads=[B_pats, B_wcc], writes=[B_wd], tag="p0wd")
            for q in range(4):
                OP("tensor", lambda e, q=q, j=j, wd_ap=wd_ap: e.matmul(
                    ent_ps[:, 512 * q:512 * (q + 1)], wd_ap,
                    lnst[:, 512 * q:512 * (q + 1)], start=(j == 0), stop=(j == 3)),
                   reads=[B_wd, B_lnst], writes=[B_entps], tag="p0mm")
            if j == 3:
                b = gt // 4
                ent_ap, B_ent = ENTt(b % 4)
                OP("vector", lambda e, ent_ap=ent_ap: e.tensor_copy(ent_ap, ent_ps),
                   reads=[B_entps], writes=[B_ent], tag="p0ev")
                for half in range(2):
                    for k8 in range(8):
                        k = 8 * half + k8
                        OP("tensor", lambda e, k=k, k8=k8, ent_ap=ent_ap: e.transpose(
                            pt_bf[:, 128 * k8:128 * (k8 + 1)],
                            ent_ap[:, 128 * k:128 * (k + 1)], ident_bf),
                           reads=[B_ent, B_identb], writes=[B_ptbf], tag="p1tr")
                    for k8 in range(8):
                        k = 8 * half + k8
                        OP("vector", lambda e, k=k, k8=k8: e.tensor_copy(
                            efm[:, 128 * k:128 * (k + 1)],
                            pt_bf[:, 128 * k8:128 * (k8 + 1)]),
                           reads=[B_ptbf], writes=[B_efm], tag="p1trev")
                OP("tensor", lambda e: e.matmul(ck_ps, ones_row, r_bkv_hi,
                                                start=True, stop=False),
                   reads=[B_ones, B_rbkvh], writes=[B_ckps])
                OP("tensor", lambda e: e.matmul(ck_ps, ones_row, r_bkv_lo,
                                                start=False, stop=False),
                   reads=[B_ones, B_rbkvl], writes=[B_ckps])
                wkv3 = wkv.rearrange("p (kk c) -> p kk c", kk=16)
                for k in range(16):
                    OP("tensor", lambda e, k=k: e.matmul(
                        ck_ps, efm[:, 128 * k:128 * (k + 1)], wkv3[:, k, :],
                        start=False, stop=(k == 15)),
                       reads=[B_efm, B_wkv], writes=[B_ckps], tag="p1mm")
                OP("vector", lambda e: e.tensor_copy(ckt, ck_ps), reads=[B_ckps],
                   writes=[B_ckt])
                OP("vector", lambda e, b=b: e.tensor_copy(
                    cv_tm[:, 128 * b:128 * (b + 1)], ckt[:, 128:256]),
                   reads=[B_ckt], writes=[B_cvtm])
                ss2 = rmsc[:, 0:1]
                OP("vector", lambda e: e.memset(ss2, 0.0), writes=[B_rmsc])
                OP("scalar", lambda e: e.activation(lnsto[:, 0:128], ckt[:, 0:128],
                                                    AF.Square, accum_out=ss2),
                   reads=[B_ckt], writes=[B_lnsto, B_rmsc], tag="p1rms")
                OP("vector", lambda e: e.tensor_scalar(ss2, ss2, 1.0 / 128, EPS_RMS,
                                                       ALU.mult, ALU.add),
                   reads=[B_rmsc], writes=[B_rmsc])
                OP("scalar", lambda e: e.activation(ss2, ss2, AF.Sqrt),
                   reads=[B_rmsc], writes=[B_rmsc])
                OP("vector", lambda e: e.reciprocal(ss2, ss2), reads=[B_rmsc],
                   writes=[B_rmsc])
                OP("vector", lambda e: e.tensor_scalar(ckt[:, 0:128], ckt[:, 0:128],
                                                       ss2, None, ALU.mult),
                   reads=[B_rmsc], writes=[B_ckt])
                OP("vector", lambda e: e.tensor_mul(ckt[:, 0:128], ckt[:, 0:128], knrep),
                   reads=[B_knrep], writes=[B_ckt])
                cosb = tabB[:, 32 * b:32 * (b + 1)]
                sinb = tabB[:, 512 + 32 * b:512 + 32 * (b + 1)]
                r0 = ckt[:, 0:32]; r1 = ckt[:, 32:64]
                OP("vector", lambda e, cosb=cosb: e.tensor_mul(ropet[:, 0:32], r0, cosb),
                   reads=[B_ckt, B_tabB], writes=[B_ropet])
                OP("vector", lambda e, sinb=sinb: e.tensor_mul(ropet[:, 32:64], r1, sinb),
                   reads=[B_ckt, B_tabB], writes=[B_ropet])
                OP("vector", lambda e, cosb=cosb: e.tensor_mul(r1, r1, cosb),
                   reads=[B_tabB], writes=[B_ckt])
                OP("vector", lambda e, sinb=sinb: e.tensor_mul(r0, r0, sinb),
                   reads=[B_tabB], writes=[B_ckt])
                OP("vector", lambda e: e.tensor_add(ckt[:, 32:64], r0, r1),
                   reads=[B_ckt], writes=[B_ckt])
                OP("vector", lambda e: e.tensor_sub(ckt[:, 0:32], ropet[:, 0:32],
                                                    ropet[:, 32:64]),
                   reads=[B_ropet], writes=[B_ckt])
                OP("vector", lambda e: e.tensor_copy(lnsto[:, 0:128], ckt[:, 0:128]),
                   reads=[B_ckt], writes=[B_lnsto])
                OP("tensor", lambda e: e.transpose(pt_bf[:, 0:128], lnsto[:, 0:128],
                                                   ident_bf),
                   reads=[B_lnsto, B_identb], writes=[B_ptbf])
                OP("vector", lambda e, b=b: e.tensor_copy(
                    ck_fm[:, 128 * b:128 * (b + 1)], pt_bf[:, 0:128]),
                   reads=[B_ptbf], writes=[B_ckfm])

        if "ck" in dbg:
            o_ck = dout("dbg_ck", [128, 2048], BF16)
            OP("sync", lambda e: e.dma_start(out=o_ck, in_=ck_fm), reads=[B_ckfm],
               dma_slot="out")
        if "cv" in dbg:
            o_cv = dout("dbg_cv", [128, 2048], BF16)
            OP("sync", lambda e: e.dma_start(out=o_cv, in_=cv_tm), reads=[B_cvtm],
               dma_slot="out")

        for sl in range(NSLOT):
            xb, B_xb = xbf[sl % 2]
            OP("gpsimd", lambda e, sl=sl, xb=xb: e.dma_start(
                out=xb, in_=d["x_own"][128 * sl:128 * (sl + 1), :]),
               writes=[B_xb], dma_slot=f"xs{sl % 2}")
            ln_tile(xb, B_xb, lnsto, B_lnsto, wlsc, B_wlsc, tag="p0own")
            for half in range(2):
                for k8 in range(8):
                    k = 8 * half + k8
                    OP("tensor", lambda e, k=k, k8=k8: e.transpose(
                        pt_bf[:, 128 * k8:128 * (k8 + 1)],
                        lnsto[:, 128 * k:128 * (k + 1)], ident_bf),
                       reads=[B_lnsto, B_identb], writes=[B_ptbf], tag="p0tr")
                for k8 in range(8):
                    k = 8 * half + k8
                    OP("vector", lambda e, k=k, k8=k8: e.tensor_copy(
                        lnf_wr[:, 128 * k:128 * (k + 1)],
                        pt_bf[:, 128 * k8:128 * (k8 + 1)]),
                       reads=[B_ptbf],
                       writes=[(B_lnfw[0], B_lnfw[1] + 256 * k,
                                B_lnfw[1] + 256 * (k + 1))], tag="p0trev")
            OP("sync", lambda e, sl=sl: e.dma_start(out=lnf_hbm[sl, :, :], in_=lnf_wr),
               reads=[B_lnfw],
               writes=[("dram_lnf", 4096 * sl, 4096 * (sl + 1))],
               dma_slot="lnw", tag="lnfw")

        ps_proj = [PF(0, 512), PF(512, 512)]
        pt2_bf = [PB(0, 1024), PB(1024, 1024)]
        ps_S = [PF(1024, 512), PF(1536, 512)]
        ps_mrg, B_psmrg = PF(0, 1024)
        ps_sl, B_pssl = PF(2048, 256)
        ps_PT = [PB(0, 1024), PB(1024, 1024)]
        ps_PLT, B_psPLT = PB(2048, 512)

        IOTA(iota_msk, B_iotam, [[1, 2048]], 0, 0)
        for t in range(TPC):
            m_ap, B_m = MSK(t)
            OP("vector", lambda e, m_ap=m_ap, t=t, W=TILE_W[t]: e.tensor_scalar(
                m_ap, iota_msk[:, 0:W], cmax_sb[:, t:t + 1], None, ALU.is_gt),
               reads=[B_iotam, B_cmax], writes=[B_m], tag="mskb")
            OP("vector", lambda e, m_ap=m_ap: e.tensor_scalar_mul(m_ap, m_ap, NEG),
               reads=[B_m], writes=[B_m], tag="mskb")

        def proj_load_bias(dsrc, g):
            OP("sync", lambda e: e.dma_start(out=brow_f,
                                             in_=dsrc[None, 512 * g:512 * (g + 1)]),
               writes=[B_browf], dma_slot="ld0")
            hi_lo_rows(brow_f, B_browf, brow_hi, B_browhi, brow_lo, B_browlo, 512)

        def proj_dma_W(dsrc, g, jbuf):
            wq_all, B_wq_all = wq_alls[jbuf]
            OP("gpsimd", lambda e, g=g, dsrc=dsrc, wq_all=wq_all: e.dma_start(
                out=wq_all.rearrange("p (k c) -> p k c", k=16),
                in_=dsrc[:, 512 * g:512 * (g + 1)].rearrange("(k p) c -> p k c", p=128)),
               writes=[B_wq_all], dma_slot=f"wq{jbuf}", tag="wdma")

        def rms_rope_store(sl, q_lo, is_q, dst_fn):
            qv = qtm[:, q_lo:q_lo + 512]
            v3 = qv.rearrange("p (h dd) -> p h dd", h=4)
            OP("vector", lambda e: e.memset(rms4, 0.0), writes=[B_rms4])
            OP("vector", lambda e, qv=qv: e.scalar_tensor_tensor(
                out=qbf[:, 512:1024], in0=qv, scalar=1.0, in1=qv,
                op0=ALU.bypass, op1=ALU.mult), reads=[B_qtm], writes=[B_qbf], tag="rr")
            OP("vector", lambda e: e.reduce_sum(
                rms4, qbf[:, 512:1024].rearrange("p (h dd) -> p h dd", h=4), axis=AX.X),
               reads=[B_qbf], writes=[B_rms4], tag="rr")
            OP("vector", lambda e: e.tensor_scalar_mul(rms4, rms4, 1.0 / 128),
               reads=[B_rms4], writes=[B_rms4])
            OP("vector", lambda e: e.tensor_scalar_add(rms4, rms4, EPS_RMS),
               reads=[B_rms4], writes=[B_rms4])
            OP("scalar", lambda e: e.activation(rms4, rms4, AF.Sqrt),
               reads=[B_rms4], writes=[B_rms4])
            OP("vector", lambda e: e.reciprocal(rms4, rms4),
               reads=[B_rms4], writes=[B_rms4])
            for hh in range(4):
                OP("vector", lambda e, v3=v3, hh=hh: e.tensor_scalar(
                    v3[:, hh, :], v3[:, hh, :], rms4[:, hh:hh + 1], None, ALU.mult),
                   reads=[B_rms4, B_qtm], writes=[B_qtm], tag="rr")
            gain = qnrep if is_q else knrep
            B_gain = B_qnrep if is_q else B_knrep
            OP("vector", lambda e, v3=v3, gain=gain: e.tensor_mul(
                v3, v3, gain.unsqueeze(1).to_broadcast([128, 4, 128])),
               reads=[B_qtm, B_gain], writes=[B_qtm], tag="rr")
            cosA = tabO[:, 32 * sl:32 * (sl + 1)]
            sinA = tabO[:, NSLOT * 32 + 32 * sl:NSLOT * 32 + 32 * (sl + 1)]
            cb = lambda t_: t_.unsqueeze(1).to_broadcast([128, 4, 32])
            r0 = v3[:, :, 0:32]; r1 = v3[:, :, 32:64]
            t0v = slsb[:, 0:128].rearrange("p (h dd) -> p h dd", h=4)
            t1v = slsb[:, 128:256].rearrange("p (h dd) -> p h dd", h=4)
            OP("vector", lambda e, r0=r0, cosA=cosA, t0v=t0v: e.tensor_mul(
                t0v, r0, cb(cosA)), reads=[B_qtm, B_tabO], writes=[B_slsb], tag="rr")
            OP("vector", lambda e, r1=r1, sinA=sinA, t1v=t1v: e.tensor_mul(
                t1v, r1, cb(sinA)), reads=[B_qtm, B_tabO], writes=[B_slsb], tag="rr")
            OP("vector", lambda e, r1=r1, cosA=cosA: e.tensor_mul(r1, r1, cb(cosA)),
               reads=[B_tabO, B_qtm], writes=[B_qtm], tag="rr")
            OP("vector", lambda e, r0=r0, sinA=sinA: e.tensor_mul(r0, r0, cb(sinA)),
               reads=[B_tabO, B_qtm], writes=[B_qtm], tag="rr")
            OP("vector", lambda e, v3=v3, r0=r0, r1=r1: e.tensor_add(
                v3[:, :, 32:64], r0, r1), reads=[B_qtm], writes=[B_qtm], tag="rr")
            OP("vector", lambda e, v3=v3, t0v=t0v, t1v=t1v: e.tensor_sub(
                v3[:, :, 0:32], t0v, t1v), reads=[B_slsb, B_qtm], writes=[B_qtm],
               tag="rr")
            OP("vector", lambda e, qv=qv: e.tensor_copy(qbf[:, 0:512], qv),
               reads=[B_qtm], writes=[B_qbf], tag="rr")
            par = sl % 2
            pt_ap, B_pt = pt2_bf[par]
            for hh in range(4):
                OP("tensor", lambda e, hh=hh, pt_ap=pt_ap: e.transpose(
                    pt_ap[:, 128 * hh:128 * (hh + 1)],
                    qbf[:, 128 * hh:128 * (hh + 1)], ident_bf),
                   reads=[B_qbf, B_identb], writes=[B_pt], tag="rrT")
            for hh in range(4):
                dap, B_d2 = dst_fn(hh)
                OP("vector", lambda e, dap=dap, hh=hh, pt_ap=pt_ap: e.tensor_copy(
                    dap, pt_ap[:, 128 * hh:128 * (hh + 1)]),
                   reads=[B_pt], writes=[B_d2], tag="rrTe")

        def lv_store(sl, q_lo):
            qv = qtm[:, q_lo:q_lo + 512]
            for hh in range(4):
                dap, B_d2 = LVG(sl, hh)
                OP("vector", lambda e, dap=dap, qv=qv, hh=hh: e.tensor_copy(
                    dap, qv[:, 128 * hh:128 * (hh + 1)]),
                   reads=[B_qtm], writes=[B_d2], tag="lve")

        def proj_group(dW, dB, g, slots, kind, jbuf):
            proj_dma_W(dW, g, jbuf)
            proj_load_bias(dB, g)
            for i, sl in enumerate(slots):
                par = i % 2
                pp, B_pp = ps_proj[par]
                stg_ap, B_stg = lnf_stg[par]
                OP("sync", lambda e, sl=sl, stg_ap=stg_ap: e.dma_start(
                    out=stg_ap, in_=lnf_hbm[sl, :, :]),
                   reads=[("dram_lnf", 4096 * sl, 4096 * (sl + 1))],
                   writes=[B_stg], dma_slot=f"lf{par}", tag="lnfr")
                OP("tensor", lambda e, pp=pp: e.matmul(pp, ones_row, brow_hi,
                                                       start=True, stop=False),
                   reads=[B_ones, B_browhi], writes=[B_pp], tag="pj")
                OP("tensor", lambda e, pp=pp: e.matmul(pp, ones_row, brow_lo,
                                                       start=False, stop=False),
                   reads=[B_ones, B_browlo], writes=[B_pp], tag="pj")
                for k in range(16):
                    wap, B_w = WQT(jbuf, k)
                    OP("tensor", lambda e, pp=pp, stg_ap=stg_ap, wap=wap,
                       k=k: e.matmul(
                        pp, stg_ap[:, 128 * k:128 * (k + 1)], wap,
                        start=False, stop=(k == 15)),
                       reads=[B_stg, B_w], writes=[B_pp], tag="pjmm")
                q_lo = 512 * par
                OP("vector", lambda e, pp=pp, q_lo=q_lo: e.tensor_copy(
                    qtm[:, q_lo:q_lo + 512], pp), reads=[B_pp], writes=[B_qtm],
                   tag="pjev")
                if kind == "q":
                    t_own = i
                    rms_rope_store(sl, q_lo, True,
                                   lambda hh, t_own=t_own: QG(hh, t_own))
                elif kind == "lk":
                    rms_rope_store(sl, q_lo, False, lambda hh, sl=sl: LKG(hh, sl))
                else:
                    lv_store(sl, q_lo)

        def attn_head(g, hh):
            h = HG * g + hh
            for pp_i in range(NPAIR):
                W = PAIR_W[pp_i]
                for ti in range(2):
                    t = 2 * pp_i + ti
                    sl_cur = 3 * pp_i + 1 + ti
                    nch = W // 512
                    qg_ap, B_qg = QG(hh, t)
                    for cc in range(nch):
                        sp, B_sp = ps_S[cc % 2]
                        OP("tensor", lambda e, sp=sp, qg_ap=qg_ap, cc=cc: e.matmul(
                            sp, qg_ap, ck_fm[:, 512 * cc:512 * (cc + 1)],
                            start=True, stop=True),
                           reads=[B_qg, B_ckfm], writes=[B_sp], tag="sc")
                        m_ap, B_m = MSK(t)
                        OP("vector", lambda e, sp=sp, m_ap=m_ap, cc=cc: e.tensor_add(
                            S_sb[:, 512 * cc:512 * (cc + 1)], sp,
                            m_ap[:, 512 * cc:512 * (cc + 1)]),
                           reads=[B_sp, B_m], writes=[B_Ssb], tag="scev")
                    OP("vector", lambda e, W=W: e.max(out=tv8, in_=S_sb[:, 0:W]),
                       reads=[B_Ssb], writes=[B_tv8], tag="max8")
                    negm = st8[:, 0:1]; t2 = st8[:, 1:2]; Zc = st8[:, 2:3]
                    rZ = st8[:, 3:4]
                    OP("vector", lambda e: e.tensor_scalar_mul(negm, tv8[:, 0:1], -1.0),
                       reads=[B_tv8], writes=[B_st8])
                    OP("vector", lambda e: e.tensor_scalar_max(t2, tv8[:, 7:8], -5.0e29),
                       reads=[B_tv8], writes=[B_st8])
                    OP("vector", lambda e: e.memset(Zc, 0.0), writes=[B_st8])
                    OP("scalar", lambda e: e.activation(qbf[:, 512:520], tv8, AF.Exp,
                                                        bias=negm, accum_out=Zc),
                       reads=[B_tv8, B_st8], writes=[B_qbf, B_st8], tag="exp8")
                    OP("vector", lambda e: e.reciprocal(rZ, Zc),
                       reads=[B_st8], writes=[B_st8])
                    OP("scalar", lambda e, W=W: e.activation(
                        Pbuf[:, 0:W], S_sb[:, 0:W], AF.Exp, bias=negm),
                       reads=[B_Ssb, B_st8], writes=[B_Pbuf], tag="expP")
                    OP("vector", lambda e, W=W: e.scalar_tensor_tensor(
                        out=Pbuf[:, 0:W], in0=S_sb[:, 0:W], scalar=t2,
                        in1=Pbuf[:, 0:W], op0=ALU.is_ge, op1=ALU.mult),
                       reads=[B_Ssb, B_st8, B_Pbuf], writes=[B_Pbuf], tag="maskP")
                    OP("vector", lambda e, W=W: e.tensor_scalar(
                        Pbuf[:, 0:W], Pbuf[:, 0:W], rZ, None, ALU.mult),
                       reads=[B_st8, B_Pbuf], writes=[B_Pbuf], tag="scaleP")
                    nkc = W // 128
                    pt_dst, B_ptd = PTv(ti, W)
                    done = 0
                    while done < nkc:
                        grp = min(8, nkc - done)
                        pta, B_pta = ps_PT[(done // 8) % 2]
                        for jj in range(grp):
                            OP("tensor", lambda e, jj=jj, dn=done, pta=pta: e.transpose(
                                pta[:, 128 * jj:128 * (jj + 1)],
                                Pbuf[:, 128 * (dn + jj):128 * (dn + jj + 1)], ident_bf),
                               reads=[B_Pbuf, B_identb], writes=[B_pta], tag="PT")
                        OP("vector", lambda e, dn=done, grp=grp, pta=pta,
                           pt_dst=pt_dst: e.tensor_copy(
                            pt_dst[:, 128 * dn:128 * (dn + grp)], pta[:, 0:128 * grp]),
                           reads=[B_pta], writes=[B_ptd], tag="PTev")
                        done += grp
                    lkw, B_lkw = LKGW(hh, sl_cur - 1)
                    OP("tensor", lambda e, qg_ap=qg_ap, lkw=lkw: e.matmul(
                        ps_sl, qg_ap, lkw, start=True, stop=True),
                       reads=[B_qg, B_lkw], writes=[B_pssl], tag="sl")
                    OP("vector", lambda e, t=t: e.tensor_add(
                        slsb, ps_sl, lmask[:, 256 * t:256 * (t + 1)]),
                       reads=[B_pssl, B_lmask], writes=[B_slsb], tag="slev")
                    ml = st8[:, 4:5]; Zl = st8[:, 5:6]; rl = st8[:, 6:7]
                    OP("vector", lambda e: e.reduce_max(ml, slsb, axis=AX.X),
                       reads=[B_slsb], writes=[B_st8])
                    OP("vector", lambda e: e.tensor_scalar_mul(ml, ml, -1.0),
                       reads=[B_st8], writes=[B_st8])
                    OP("vector", lambda e: e.memset(Zl, 0.0), writes=[B_st8])
                    OP("scalar", lambda e: e.activation(plsb, slsb, AF.Exp, bias=ml,
                                                        accum_out=Zl),
                       reads=[B_slsb, B_st8], writes=[B_plsb, B_st8], tag="expL")
                    OP("vector", lambda e: e.reciprocal(rl, Zl),
                       reads=[B_st8], writes=[B_st8])
                    OP("vector", lambda e: e.tensor_scalar(plsb, plsb, rl, None,
                                                           ALU.mult),
                       reads=[B_st8, B_plsb], writes=[B_plsb], tag="plm")
                    for jj in range(2):
                        OP("tensor", lambda e, jj=jj: e.transpose(
                            ps_PLT[:, 128 * jj:128 * (jj + 1)],
                            plsb[:, 128 * jj:128 * (jj + 1)], ident_bf),
                           reads=[B_plsb, B_identb], writes=[B_psPLT], tag="PLT")
                    OP("vector", lambda e, ti=ti: e.tensor_copy(
                        pltsb[:, 256 * ti:256 * (ti + 1)], ps_PLT[:, 0:256]),
                       reads=[B_psPLT], writes=[B_pltsb], tag="PLTev")
                nkc = W // 128
                for kc in range(nkc):
                    for ti in range(2):
                        t = 2 * pp_i + ti
                        bank_first = (pp_i % 2 == 0) and (kc == 0) and (ti == 0)
                        pt_dst, B_ptd = PTv(ti, W)
                        OP("tensor", lambda e, kc=kc, t=t, pt_dst=pt_dst,
                           st=bank_first: e.matmul(
                            ps_mrg[:, 128 * t:128 * (t + 1)],
                            cv_tm[:, 128 * kc:128 * (kc + 1)],
                            pt_dst[:, 128 * kc:128 * (kc + 1)],
                            start=st, stop=False, skip_group_check=True),
                           reads=[B_cvtm, B_ptd], writes=[B_psmrg], tag="ctx")
                for ti in range(2):
                    t = 2 * pp_i + ti
                    sl_cur = 3 * pp_i + 1 + ti
                    for jj, sl_kv in enumerate((sl_cur - 1, sl_cur)):
                        bank_last = (pp_i % 2 == 1) and (ti == 1) and (jj == 1)
                        lv_ap, B_lv = LVG(sl_kv, hh)
                        OP("tensor", lambda e, lv_ap=lv_ap, jj=jj, t=t, ti=ti,
                           sp=bank_last: e.matmul(
                            ps_mrg[:, 128 * t:128 * (t + 1)], lv_ap,
                            pltsb[:, 256 * ti + 128 * jj:256 * ti + 128 * (jj + 1)],
                            start=False, stop=sp, skip_group_check=True),
                           reads=[B_lv, B_pltsb], writes=[B_psmrg], tag="lctx")
            OP("vector", lambda e: e.tensor_scalar_mul(mrgb, ps_mrg, 0.5),
               reads=[B_psmrg], writes=[B_mrgb], tag="mgev")
            OP("sync", lambda e, h=h: e.dma_start(out=merged_hbm[h, :, :], in_=mrgb),
               reads=[B_mrgb], writes=[("dram_mrg", h * 1024, (h + 1) * 1024)],
               dma_slot="mrg")

        pj_i = 0
        for g in range(NHG):
            own_slots_q = [3 * (t // 2) + 1 + (t % 2) for t in range(TPC)]
            for dW, dB, slots, kind in (
                    (d["Wq"], d["bq"], own_slots_q, "q"),
                    (d["Wlk"], d["blk"], list(range(NSLOT)), "lk"),
                    (d["Wlv"], d["blv"], list(range(NSLOT)), "lv")):
                proj_group(dW, dB, g, slots, kind, pj_i % 2)
                pj_i += 1
            for hh in range(HG):
                attn_head(g, hh)

        OP("gpsimd", lambda e: e.dma_start(
            out=wo_all.rearrange("p (k c) -> p k c", k=16),
            in_=d["Wo"].rearrange("(k p) c -> p k c", p=128)),
           writes=[B_WO_ALL], dma_slot="wo", tag="wodma")
        OP("sync", lambda e: e.dma_start(out=bo_f, in_=d["bo"][None, :]),
           writes=[B_bof], dma_slot="ld0")
        hi_lo_rows(bo_f, B_bof, bo_hi, B_bohi, bo_lo, B_bolo, 2048)
        pbr, B_pbr = PF(0, 2048)
        for q in range(4):
            OP("tensor", lambda e, q=q: e.matmul(pbr[:, 512 * q:512 * (q + 1)], ones_row,
                                                 bo_hi[:, 512 * q:512 * (q + 1)],
                                                 start=True, stop=False),
               reads=[B_ones, B_bohi], writes=[B_pbr])
            OP("tensor", lambda e, q=q: e.matmul(pbr[:, 512 * q:512 * (q + 1)], ones_row,
                                                 bo_lo[:, 512 * q:512 * (q + 1)],
                                                 start=False, stop=True),
               reads=[B_ones, B_bolo], writes=[B_pbr])
        OP("vector", lambda e: e.tensor_copy(borep, pbr), reads=[B_pbr], writes=[B_borep])

        ps_wo, B_pswo = PF(0, 2048)
        for t in range(TPC):
            ms_ap, B_ms = mstg[t % 2]
            OP("sync", lambda e, t=t, ms_ap=ms_ap: e.dma_start(
                out=ms_ap.rearrange("p (h c) -> p h c", h=16),
                in_=merged_hbm[:, :, 128 * t:128 * (t + 1)].transpose([1, 0, 2])),
               reads=[("dram_mrg", 0, NH * 1024)], writes=[B_ms],
               dma_slot=f"ms{t % 2}", tag="msdma")
            xb, B_xb = xst[0]
            sl_x = 3 * (t // 2) + 1 + (t % 2)
            OP("sync", lambda e, sl_x=sl_x, xb=xb: e.dma_start(
                out=xb, in_=d["x_own"][128 * sl_x:128 * (sl_x + 1), :]),
               writes=[B_xb], dma_slot="xs0", tag="x3dma")
            for h in range(NH):
                wo_ap, B_woc = WOc(h)
                for b4 in range(4):
                    OP("tensor", lambda e, h=h, b4=b4, ms_ap=ms_ap,
                       wo_ap=wo_ap: e.matmul(
                        ps_wo[:, 512 * b4:512 * (b4 + 1)],
                        ms_ap[:, 128 * h:128 * (h + 1)],
                        wo_ap[:, 512 * b4:512 * (b4 + 1)],
                        start=(h == 0), stop=(h == NH - 1)),
                       reads=[B_ms, B_woc], writes=[B_pswo], tag="womm")
            x2b, B_x2b = xst[1]
            OP("vector", lambda e, x2b=x2b, xb=xb: e.tensor_add(x2b, ps_wo, xb),
               reads=[B_pswo, B_xb], writes=[B_x2b], tag="x2ev")
            OP("vector", lambda e, x2b=x2b: e.tensor_add(x2b, x2b, borep),
               reads=[B_borep], writes=[B_x2b], tag="x2ev")
            OP("sync", lambda e, t=t, x2b=x2b: e.dma_start(
                out=y[128 * t:128 * (t + 1), :], in_=x2b),
               reads=[B_x2b], writes=[("dram_y", 0, 1)], dma_slot="yw", tag="ywr")
            ln_tile(x2b, B_x2b, ln2st, B_ln2st, wlsc, B_wlsc, tag="ln2")
            for half in range(2):
                for k8 in range(8):
                    k = 8 * half + k8
                    OP("tensor", lambda e, k=k, k8=k8: e.transpose(
                        pt_bf[:, 128 * k8:128 * (k8 + 1)],
                        ln2st[:, 128 * k:128 * (k + 1)], ident_bf),
                       reads=[B_ln2st, B_identb], writes=[B_ptbf], tag="ln2tr")
                for k8 in range(8):
                    k = 8 * half + k8
                    dap, B_d2 = LN2c(k)
                    OP("vector", lambda e, dap=dap, k8=k8, t=t: e.tensor_copy(
                        dap[:, 128 * t:128 * (t + 1)], pt_bf[:, 128 * k8:128 * (k8 + 1)]),
                       reads=[B_ptbf],
                       writes=[(B_d2[0], B_d2[1] + 256 * t, B_d2[1] + 256 * (t + 1))],
                       tag="ln2te")

        ps_cs, B_pscs = PF(2048, 512)

        def colsum_rows(src_chunks, nk, tag):
            for bb in range(2):
                for k in range(nk):
                    cap, B_c = src_chunks(k)
                    OP("tensor", lambda e, cap=cap, bb=bb, k=k: e.matmul(
                        ps_cs[0:1, 0:512], ones128[:, 0:1],
                        cap[:, 512 * bb:512 * (bb + 1)],
                        start=(k == 0), stop=(k == nk - 1)),
                       reads=[B_c, B_ones128], writes=[B_pscs], tag=tag)
                OP("vector", lambda e, bb=bb: e.tensor_scalar_mul(
                    cs_f[:, 512 * bb:512 * (bb + 1)], ps_cs[0:1, 0:512], -128.0),
                   reads=[B_pscs], writes=[B_csf], tag=tag)
            OP("vector", lambda e: e.tensor_copy(cs_f2, cs_f), reads=[B_csf],
               writes=[B_csf2], tag=tag)
            hi_lo_rows(cs_f2, B_csf2, cs_hi, B_cshi, cs_lo, B_cslo, 1024)

        ps_g, B_psg = PF(0, 1024)
        ps_u, B_psu = PF(1024, 1024)
        ps_d = [PF(0, 1024), PF(1024, 1024)]
        ps_tr = [PF(2048, 512), PF(2048, 512)]

        if run_p4:
            colsum_rows(LN2c, 16, "cs1")
            for m in range(44):
                wg_ap, B_wg = wgbuf[m % 2]
                for which, dW4 in (("g", d["wgT"]), ("u", d["wuT"])):
                    OP("gpsimd", lambda e, m=m, dW4=dW4, wg_ap=wg_ap: e.dma_start(
                        out=wg_ap.rearrange("p (k c) -> p k c", k=16),
                        in_=dW4[:, 128 * m:128 * (m + 1)].rearrange(
                            "(k p) c -> p k c", p=128)),
                       writes=[B_wg], dma_slot=f"w{which}{m % 2}", tag="ffnwdma")
                    pp, B_pp = (ps_g, B_psg) if which == "g" else (ps_u, B_psu)
                    for bb in range(2):
                        OP("tensor", lambda e, bb=bb, pp=pp: e.matmul(
                            pp[:, 512 * bb:512 * (bb + 1)], ones_row,
                            cs_hi[:, 512 * bb:512 * (bb + 1)], start=True, stop=False),
                           reads=[B_ones, B_cshi], writes=[B_pp], tag="gseed")
                        OP("tensor", lambda e, bb=bb, pp=pp: e.matmul(
                            pp[:, 512 * bb:512 * (bb + 1)], ones_row,
                            cs_lo[:, 512 * bb:512 * (bb + 1)], start=False, stop=False),
                           reads=[B_ones, B_cslo], writes=[B_pp], tag="gseed")
                    for k in range(16):
                        lap, B_l2 = LN2c(k)
                        for bb in range(2):
                            OP("tensor", lambda e, k=k, bb=bb, wg_ap=wg_ap, lap=lap,
                               pp=pp: e.matmul(
                                pp[:, 512 * bb:512 * (bb + 1)],
                                wg_ap[:, 128 * k:128 * (k + 1)],
                                lap[:, 512 * bb:512 * (bb + 1)],
                                start=False, stop=(k == 15)),
                               reads=[B_wg, B_l2], writes=[B_pp], tag="gumm")
                    if which == "g":
                        OP("scalar", lambda e, m=m: e.activation(
                            silu_g, ps_g, AF.Silu, scale=sgup[:, m:m + 1]),
                           reads=[B_psg, B_sgup], writes=[B_silug], tag="silu")
                    else:
                        gu_ap, B_gu = GU(m)
                        OP("vector", lambda e, m=m, gu_ap=gu_ap: e.scalar_tensor_tensor(
                            out=gu_ap, in0=ps_u, scalar=sgup[:, 44 + m:45 + m],
                            in1=silu_g, op0=ALU.mult, op1=ALU.mult),
                           reads=[B_psu, B_sgup, B_silug], writes=[B_gu], tag="guev")

            colsum_rows(GU, 44, "cs2")
            for m in range(16):
                wd_ap, B_wd = wdbuf[m % 2]
                OP("gpsimd", lambda e, m=m, wd_ap=wd_ap: e.dma_start(
                    out=wd_ap.rearrange("p (k c) -> p k c", k=44),
                    in_=d["wdT"][:, 128 * m:128 * (m + 1)].rearrange(
                        "(k p) c -> p k c", p=128)),
                   writes=[B_wd], dma_slot=f"wd{m % 2}", tag="wddma")
                pp, B_pp = ps_d[m % 2]
                for bb in range(2):
                    OP("tensor", lambda e, bb=bb, pp=pp: e.matmul(
                        pp[:, 512 * bb:512 * (bb + 1)], ones_row,
                        cs_hi[:, 512 * bb:512 * (bb + 1)], start=True, stop=False),
                       reads=[B_ones, B_cshi], writes=[B_pp], tag="dseed")
                    OP("tensor", lambda e, bb=bb, pp=pp: e.matmul(
                        pp[:, 512 * bb:512 * (bb + 1)], ones_row,
                        cs_lo[:, 512 * bb:512 * (bb + 1)], start=False, stop=False),
                       reads=[B_ones, B_cslo], writes=[B_pp], tag="dseed")
                for k in range(44):
                    gu_ap, B_gu = GU(k)
                    for bb in range(2):
                        OP("tensor", lambda e, k=k, bb=bb, wd_ap=wd_ap, gu_ap=gu_ap,
                           pp=pp: e.matmul(
                            pp[:, 512 * bb:512 * (bb + 1)],
                            wd_ap[:, 128 * k:128 * (k + 1)],
                            gu_ap[:, 512 * bb:512 * (bb + 1)],
                            start=False, stop=(k == 43)),
                           reads=[B_wd, B_gu], writes=[B_pp], tag="dmm")
                OP("vector", lambda e, m=m, pp=pp: e.tensor_scalar(
                    dstg, pp, sdown[:, m:m + 1], None, ALU.mult),
                   reads=[B_pp, B_sdown], writes=[B_dstg], tag="dsc")
                for half in range(2):
                    pt_ap, B_pt4 = ps_tr[half]
                    for jj in range(4):
                        tt4 = 4 * half + jj
                        OP("tensor", lambda e, tt4=tt4, jj=jj, pt_ap=pt_ap: e.transpose(
                            pt_ap[:, 128 * jj:128 * (jj + 1)],
                            dstg[:, 128 * tt4:128 * (tt4 + 1)], ident_f),
                           reads=[B_dstg, B_identf], writes=[B_pt4], tag="dtr")
                    OP("vector", lambda e, half=half, pt_ap=pt_ap: e.tensor_copy(
                        ytile[:, 512 * half:512 * (half + 1)], pt_ap),
                       reads=[B_pt4], writes=[B_yt], tag="dtrev")
                OP("gpsimd", lambda e, m=m: e.dma_start(
                    out=y.rearrange("(t p) c -> p t c", p=128)[:, :, 128 * m:128 * (m + 1)],
                    in_=ytile.rearrange("p (t c) -> p t c", t=8),
                    accum_op=ALU.add),
                   reads=[B_yt, ("dram_y", 0, 1)], writes=[("dram_y", 0, 1)],
                   dma_slot="yacc", tag="yacc")

        with nc.Block() as block:
            prog.finalize(block)
        for cm in prog._sem_ctxs:
            cm.__exit__(None, None, None)

    return nc, sorted(dbg_outs.keys())


def _host_tables():
    invf = np.exp(-np.arange(32, dtype=np.float64) * np.log(10000.0) / 32.0)
    return invf


def _prep_inputs(core, raw):
    x = np.ascontiguousarray(np.asarray(raw["L0_attn_hidden_states"],
                                        dtype=np.float32).reshape(S, HID))
    slots = slot_global_tiles(core)
    x_own = np.ascontiguousarray(
        np.concatenate([x[128 * g:128 * (g + 1), :] for g in slots], axis=0))
    pos_own = np.concatenate([np.arange(128 * g, 128 * (g + 1), dtype=np.float64)
                              for g in slots])
    invf = _host_tables()
    posT_own = np.ascontiguousarray(
        np.broadcast_to(pos_own[None, :], (32, NSLOT * 128))).astype(np.float32)
    kw_own = np.round(invf[:, None] * pos_own[None, :] / (2 * np.pi)
                      ).astype(np.float32)
    ends = (np.arange(1, C + 1, dtype=np.float64) * 4 - 1)
    posB = np.ascontiguousarray(
        np.broadcast_to(ends[None, :], (32, C))).astype(np.float32)
    kwB = np.round(invf[:, None] * ends[None, :] / (2 * np.pi)).astype(np.float32)

    own = own_global_tiles(core)
    cmax = np.empty((128, TPC), np.float32)
    p = np.arange(128)
    for j, g in enumerate(own):
        cmax[:, j] = np.floor((128 * g + p - 3) / 4.0).astype(np.float32)
    lm = np.zeros((TPC, 128, 256), np.float32)
    i = np.arange(128)[:, None]
    jj = np.arange(256)[None, :]
    band = (jj >= i) & (jj <= i + 128)
    for t, g in enumerate(own):
        m = band.copy()
        if g == 0:
            m = m & (jj >= 128)
        lm[t] = np.where(m, 0.0, NEG).astype(np.float32)
    lmask = np.ascontiguousarray(lm.transpose(1, 0, 2).reshape(128, TPC * 256))

    inp = {
        "x": x, "x_own": x_own, "posT_own": posT_own, "kw_own": kw_own,
        "posB": posB, "kwB": kwB, "cmax": cmax, "lmask": lmask,
        "Wq": np.asarray(raw["L0_attn_Wq"], np.float32),
        "bq": np.asarray(raw["L0_attn_bq"], np.float32),
        "Wcmp": np.asarray(raw["L0_attn_Wcmp"], np.float32),
        "bcmp": np.asarray(raw["L0_attn_bcmp"], np.float32),
        "Wk": np.asarray(raw["L0_attn_Wk"], np.float32),
        "bk": np.asarray(raw["L0_attn_bk"], np.float32),
        "Wv": np.asarray(raw["L0_attn_Wv"], np.float32),
        "bv": np.asarray(raw["L0_attn_bv"], np.float32),
        "Wlk": np.asarray(raw["L0_attn_Wlk"], np.float32),
        "blk": np.asarray(raw["L0_attn_blk"], np.float32),
        "Wlv": np.asarray(raw["L0_attn_Wlv"], np.float32),
        "blv": np.asarray(raw["L0_attn_blv"], np.float32),
        "qn_w": np.asarray(raw["L0_attn_qn_w"], np.float32),
        "kn_w": np.asarray(raw["L0_attn_kn_w"], np.float32),
        "Wo": np.asarray(raw["L0_attn_Wo"], np.float32),
        "bo": np.asarray(raw["L0_attn_bo"], np.float32),
        "wgT": np.ascontiguousarray(np.asarray(raw["L0_ffn_wq_gate"], np.int32).T),
        "s_gate": np.asarray(raw["L0_ffn_s_gate"], np.float32),
        "wuT": np.ascontiguousarray(np.asarray(raw["L0_ffn_wq_up"], np.int32).T),
        "s_up": np.asarray(raw["L0_ffn_s_up"], np.float32),
        "wdT": np.ascontiguousarray(np.asarray(raw["L0_ffn_wq_down"], np.int32).T),
        "s_down": np.asarray(raw["L0_ffn_s_down"], np.float32),
    }
    return inp


def kernel(**raw):
    nc, _ = build_program()
    in_maps = [_prep_inputs(c, raw) for c in range(NCORE)]
    res = run_bass_kernel_spmd(nc, in_maps, core_ids=list(range(NCORE)))
    y = np.zeros((S, HID), np.float32)
    for c in range(NCORE):
        shard = res.results[c]["y"]
        for j, g in enumerate(own_global_tiles(c)):
            y[128 * g:128 * (g + 1), :] = shard[128 * j:128 * (j + 1), :]
    return y.reshape(1, S, HID)
